# version 28
# baseline (speedup 1.0000x reference)
import numpy as np
import concourse.bass as bass
import concourse.mybir as mybir
from concourse.alu_op_type import AluOpType as ALU
from concourse.bass_utils import run_bass_kernel_spmd
from contextlib import ExitStack

F32 = mybir.dt.float32
BF16 = mybir.dt.bfloat16
I16 = mybir.dt.int16
AF = mybir.ActivationFunctionType
AX = mybir.AxisListType

C32 = 12582912.0
D, E, H = 1024, 256, 4096
KD, EC, KH, NH = 8, 2, 32, 8
W1C, W2C = 32, 8
EPS_LN = 1e-5
EPS_Q = 1e-5
EPS_Z = 1e-8

N_CORES = 8

C_S1, C_S2, C_TH1P, C_TH2P, C_TH1N, C_TH2N = 32, 33, 34, 35, 36, 37


class _FakeInstr:
    def then_inc(self, *a, **k):
        return self


_FAKE = _FakeInstr()


class _Proxy:
    def __init__(self, eng):
        self._e = eng

    def __getattr__(self, name):
        if self._e is None:
            return lambda *a, **k: _FAKE
        return getattr(self._e, name)


class Sem:
    def __init__(self, handle):
        self.h = handle
        self.c = 0


def build_nc(n_seq, T):
    TC = T // 128
    assert TC % 2 == 0
    NCH = n_seq * TC
    nc = bass.Bass("TRN2", target_bir_lowering=False)

    x_d = nc.dram_tensor("x", [n_seq, T, D], F32, kind="ExternalInput").ap()
    qp_d = nc.dram_tensor("q_proj", [D, E], F32, kind="ExternalInput").ap()
    kp_d = nc.dram_tensor("k_proj", [D, E], F32, kind="ExternalInput").ap()
    w1_d = nc.dram_tensor("w1", [H, D], F32, kind="ExternalInput").ap()
    w2_d = nc.dram_tensor("w2", [D, H], F32, kind="ExternalInput").ap()
    out_d = nc.dram_tensor("out", [n_seq, T, D], F32, kind="ExternalOutput").ap()

    xn_scr = nc.dram_tensor("xn_scr", [TC, KD, 128, 128], BF16).ap()
    xnq_scr = nc.dram_tensor("xnq_scr", [2, TC, KD, 128, 128], BF16).ap()
    q_scr = nc.dram_tensor("q_scr", [n_seq, TC, EC, 128, 128], BF16).ap()
    hq_scr = nc.dram_tensor("hq_scr", [n_seq, TC, KH, 128, 128], BF16).ap()
    wq1_scr = nc.dram_tensor("wq1_scr", [H, D], BF16).ap()
    wq2_scr = nc.dram_tensor("wq2_scr", [D, H], BF16).ap()

    st = ExitStack()
    EE = st.enter_context
    with st:
        wqT = EE(nc.sbuf_tensor("wqT", [128, KH, 1024], BF16))
        qkb = EE(nc.sbuf_tensor("qkb", [128, 2, KD, E], BF16))
        kvb = EE(nc.sbuf_tensor("kvb", [128, n_seq, EC, 1025], BF16))
        xb = EE(nc.sbuf_tensor("xb", [128, 2, D], F32))
        xa0 = EE(nc.sbuf_tensor("xa0", [128, 2, D], F32))
        xnb = EE(nc.sbuf_tensor("xnb", [128, 2, D], BF16))
        xnTb = EE(nc.sbuf_tensor("xnTb", [128, 2, D], BF16))
        xqTb = EE(nc.sbuf_tensor("xqTb", [128, 2, D], BF16))
        cb = EE(nc.sbuf_tensor("cb", [128, D], I16))
        dbuf = EE(nc.sbuf_tensor("dbuf", [128, H], BF16))
        xnqb = EE(nc.sbuf_tensor("xnqb", [128, 2, D], BF16))
        qeb = EE(nc.sbuf_tensor("qeb", [128, 2, E], BF16))
        keb = EE(nc.sbuf_tensor("keb", [128, 2, E], BF16))
        qTb = EE(nc.sbuf_tensor("qTb", [128, 2, E], BF16))
        ob = EE(nc.sbuf_tensor("ob", [128, D], F32))
        cols = EE(nc.sbuf_tensor("cols", [128, 64], F32))
        statc = EE(nc.sbuf_tensor("statc", [128, 2, 4, TC], F32))
        dcols = EE(nc.sbuf_tensor("dcols", [128, 2, 4, TC], F32))
        ccst = EE(nc.sbuf_tensor("ccst", [128, 4], F32))
        acolT = EE(nc.sbuf_tensor("acolT", [128, NCH], F32))
        bcolT = EE(nc.sbuf_tensor("bcolT", [128, NCH], F32))
        zrb = EE(nc.sbuf_tensor("zrb", [128, 2], F32))
        wcols = EE(nc.sbuf_tensor("wcols", [128, 40], F32))
        onescol = EE(nc.sbuf_tensor("onescol", [128, 1], F32))
        onesrow = EE(nc.sbuf_tensor("onesrow", [1, 128], F32))
        sval = EE(nc.sbuf_tensor("sval", [1, 4], F32))

        ARENA = 57344
        ph = EE(nc.sbuf_tensor("arena", [128, ARENA], mybir.dt.uint8))
        base = nc.lookup_mloc(ph).addr
        wb_t = nc.alloc_sbuf_tensor_at("wb", [128, 2, D], F32, offset=base)
        wb2_t = nc.alloc_sbuf_tensor_at("wb2", [128, H], F32, offset=base + 8192)
        wsg_t = nc.alloc_sbuf_tensor_at("wsg", [128, 2, H], BF16, offset=base + 24576)
        wqo_t = nc.alloc_sbuf_tensor_at("wqo", [128, 2, H], BF16, offset=base + 40960)
        wb, wb2, wsg, wqo = wb_t.ap(), wb2_t.ap(), wsg_t.ap(), wqo_t.ap()
        hb_t = nc.alloc_sbuf_tensor_at("hb", [128, 2, H], F32, offset=base)
        hqTb_t = nc.alloc_sbuf_tensor_at("hqTb", [128, 2, H], BF16, offset=base)
        chb_t = nc.alloc_sbuf_tensor_at("chb", [128, H], I16, offset=base + 32768)
        hqb_t = nc.alloc_sbuf_tensor_at("hqb", [128, H], BF16, offset=base + 40960)
        hb, chb, hqb = hb_t.ap(), chb_t.ap(), hqb_t.ap()
        hqTb = hqTb_t.ap()

        ppq = EE(nc.psum_tensor("ppq", [128, 512], F32))
        ppk = EE(nc.psum_tensor("ppk", [128, 512], F32))
        pkv = EE(nc.psum_tensor("pkv", [128, 2, 1024], F32))
        pf1 = EE(nc.psum_tensor("pf1", [128, 2, 512], F32))

        def SS(name):
            return Sem(EE(nc.semaphore(name)))
        s_x = [SS("s_x0"), SS("s_x1")]
        s_xA = [SS("s_xA0"), SS("s_xA1")]
        s_x2 = [SS("s_x20"), SS("s_x21")]
        s_w = [SS("s_w0"), SS("s_w1")]
        s_wp2 = [SS("s_wp20"), SS("s_wp21")]
        s_wb2a = SS("s_wb2a")
        s_wb2b = SS("s_wb2b")
        s_qk = SS("s_qk")
        s_xnw = [SS("s_xnw0"), SS("s_xnw1")]
        s_xqw = [SS("s_xqw0"), SS("s_xqw1")]
        s_xnT = [SS("s_xnT0"), SS("s_xnT1")]
        s_xqT = [SS("s_xqT0"), SS("s_xqT1")]
        s_qw = [SS("s_qw0"), SS("s_qw1")]
        s_hqw = [SS("s_hqw0"), SS("s_hqw1")]
        s_hqT = [SS("s_hqT0"), SS("s_hqT1")]
        s_qT = [SS("s_qT0"), SS("s_qT1")]
        s_w1q = [SS("s_w1q0"), SS("s_w1q1")]
        s_w2q = [SS("s_w2q0"), SS("s_w2q1")]
        s_wT = SS("s_wT")
        s_st = [SS("s_st0"), SS("s_st1")]
        s_V = SS("s_V")
        s_S = SS("s_S")
        s_T = SS("s_T")
        s_G = SS("s_G")
        allsems = [s_qk, s_wT, s_V, s_S, s_T, s_G, s_wb2a, s_wb2b] + s_x + s_xA + \
            s_x2 + s_w + s_wp2 + s_xnw + s_xqw + s_xnT + s_xqT + s_qw + s_hqw + \
            s_hqT + s_qT + s_w1q + s_w2q + s_st

        blk = EE(nc.Block())

        col = lambda i: cols[:, i:i + 1]

        def _fidx(s, m):
            return (2 * m + s) if s < 2 else (16 * s + m)

        def _fpar(s, m):
            return _fidx(s, m) % 2

        def run_schedule(sync, vec, sca, ten, gp):
            for s in allsems:
                s.c = 0
            ms = {"wscols": 0}
            state = {"xb_hist": [], "a0_hist": [], "c_prev": None, "f_prev": [],
                     "a1_b": {}}

            def inc(instr, sem, n=1):
                instr.then_inc(sem.h, n)
                sem.c += n

            def mark(tag, sem):
                ms[tag] = sem.c

            def w(eng, sem, tag):
                eng.wait_ge(sem.h, ms[tag])

            def xb_load(eng_sems, kind, s, m, src):
                hist = state["xb_hist"]
                if len(hist) >= 2:
                    pk, ps, pm = hist[-2]
                    if pk == "A1":
                        w(sync, s_V, f"xnq_{ps}_{pm}")
                    else:
                        w(sync, s_V, f"ep1_{ps}_{pm}")
                b = len(hist) % 2
                hist.append((kind, s, m))
                inc(sync.dma_start(out=xb[:, b, :], in_=src), eng_sems[b], 16)
                return b

            def emit_A0(s, m):
                hist0 = state["a0_hist"]
                b = len(hist0) % 2
                if len(hist0) >= 2:
                    ps, pm = hist0[-2]
                    w(sync, s_V, f"lnst_{ps}_{pm}")
                    w(sync, s_S, f"ssq_{ps}_{pm}")
                hist0.append((s, m))
                inc(sync.dma_start(out=xa0[:, b, :], in_=x_d[s, m * 128:(m + 1) * 128, :]),
                    s_x[b], 16)
                mark(f"x0_{s}_{m}", s_x[b])
                ss = s % 2
                w(vec, s_x[b], f"x0_{s}_{m}")
                if m == 0 and s >= 2:
                    w(vec, s_V, f"batch_{s - 2}")
                xv = xa0[:, b, :]
                vec.tensor_reduce(statc[:, ss, 1, m:m + 1], xv, axis=AX.X, op=ALU.max)
                inc(vec.tensor_reduce(statc[:, ss, 2, m:m + 1], xv, axis=AX.X, op=ALU.min), s_V)
                mark(f"lnst_{s}_{m}", s_V)
                w(sca, s_x[b], f"x0_{s}_{m}")
                if m == 0 and s >= 2:
                    w(sca, s_V, f"batch_{s - 2}")
                sca.activation(out=ob[:], in_=xv, func=AF.Identity, bias=ccst[:, 0:1],
                               scale=1.0, accum_out=statc[:, ss, 0, m:m + 1])
                sca.drain()
                sca.activation(out=ob[:], in_=xv, func=AF.Square, bias=ccst[:, 0:1],
                               scale=1.0, accum_out=statc[:, ss, 3, m:m + 1])
                inc(sca.drain(), s_S)
                mark(f"ssq_{s}_{m}", s_S)

            def emit_batch(s):
                ss = s % 2
                sc_ = lambda i: statc[:, ss, i, :]
                dc = lambda i: dcols[:, ss, i, :]
                mneg = cols[:, 0:TC]
                tmp = cols[:, 16:16 + TC]
                w(vec, s_V, f"lnst_{s}_{TC - 1}")
                w(vec, s_S, f"ssq_{s}_{TC - 1}")
                if s >= 2:
                    w(vec, s_S, f"xnu_{s - 2}_{TC - 1}")
                vec.tensor_scalar(out=mneg, in0=sc_(0), scalar1=-1.0 / D, scalar2=None, op0=ALU.mult)
                vec.drain()
                vec.tensor_tensor(out=tmp, in0=mneg, in1=mneg, op=ALU.mult)
                vec.drain()
                vec.scalar_tensor_tensor(out=dc(0), in0=sc_(3), scalar=1.0 / D,
                                         in1=tmp, op0=ALU.mult, op1=ALU.subtract)
                vec.drain()
                inc(vec.tensor_scalar(out=dc(0), in0=dc(0), scalar1=EPS_LN, scalar2=None, op0=ALU.add), s_V)
                mark(f"vpe_{s}", s_V)
                w(sca, s_V, f"vpe_{s}")
                sca.activation(out=tmp, in_=dc(0), func=AF.Ln, bias=ccst[:, 0:1], scale=1.0)
                sca.drain()
                inc(sca.activation(out=dc(0), in_=tmp, func=AF.Exp,
                                   bias=ccst[:, 0:1], scale=-0.5), s_S)
                mark(f"rstd_{s}", s_S)
                w(vec, s_S, f"rstd_{s}")
                vec.tensor_tensor(out=tmp, in0=sc_(1), in1=mneg, op=ALU.add)
                vec.drain()
                vec.scalar_tensor_tensor(out=sc_(2), in0=sc_(2), scalar=-1.0,
                                         in1=mneg, op0=ALU.mult, op1=ALU.subtract)
                vec.drain()
                vec.tensor_tensor(out=tmp, in0=tmp, in1=sc_(2), op=ALU.max)
                vec.drain()
                vec.tensor_tensor(out=sc_(1), in0=tmp, in1=dc(0), op=ALU.mult)
                vec.drain()
                vec.tensor_scalar(out=sc_(1), in0=sc_(1), scalar1=EPS_Q, scalar2=None, op0=ALU.max)
                vec.drain()
                vec.reciprocal(sc_(0), sc_(1))
                vec.drain()
                vec.scalar_tensor_tensor(out=dc(1), in0=sc_(0), scalar=-127.0,
                                         in1=dc(0), op0=ALU.mult, op1=ALU.mult)
                vec.drain()
                vec.tensor_tensor(out=dc(2), in0=mneg, in1=dc(1), op=ALU.mult)
                vec.tensor_tensor(out=dc(3), in0=mneg, in1=dc(0), op=ALU.mult)
                vec.drain()
                vec.tensor_scalar(out=dc(2), in0=dc(2), scalar1=0.5, scalar2=None, op0=ALU.add)
                if s >= 2:
                    vec.tensor_scalar(out=acolT[:, s * TC:(s + 1) * TC], in0=sc_(1),
                                      scalar1=col(C_S1), scalar2=-1.0 / 254.0, op0=ALU.mult, op1=ALU.mult)
                inc(vec.drain(), s_V)
                mark(f"batch_{s}", s_V)

            def emit_A1a(s, m):
                b = xb_load(s_xA, "A1", s, m, x_d[s, m * 128:(m + 1) * 128, :])
                mark(f"xA_{s}_{m}", s_xA[b])
                seq = s * TC + m
                ss = s % 2
                dc = lambda i: dcols[:, ss, i, m:m + 1]
                state["a1_b"][(s, m)] = b
                prev2 = divmod(seq - 2, TC) if seq >= 2 else None
                w(sca, s_xA[b], f"xA_{s}_{m}")
                w(sca, s_V, f"batch_{s}")
                if prev2 is not None:
                    w(sca, s_xnw[b], f"xnw_{prev2[0]}_{prev2[1]}")
                    w(sca, s_T, f"kv_{prev2[0]}_{prev2[1]}")
                xv = xb[:, b, :]
                sca.activation(out=xnb[:, b, :], in_=xv, func=AF.Identity,
                               bias=dc(3), scale=dc(0))
                sca.activation(out=xv, in_=xv, func=AF.Identity,
                               bias=dc(2), scale=dc(1))
                inc(sca.drain(), s_S)
                mark(f"xnu_{s}_{m}", s_S)
                w(vec, s_S, f"xnu_{s}_{m}")
                vec.drain()
                vec.tensor_scalar(out=cb[:], in0=xv, scalar1=C32,
                                  scalar2=C32 + 1.0, op0=ALU.add, op1=ALU.subtract)
                vec.drain()
                vec.tensor_tensor(out=xv, in0=xv, in1=cb[:], op=ALU.subtract)
                vec.drain()
                if prev2 is not None:
                    w(vec, s_xqw[b], f"xqw_{prev2[0]}_{prev2[1]}")
                inc(vec.scalar_tensor_tensor(out=xnqb[:, b, :], in0=xv, scalar=1.0,
                                             in1=cb[:], op0=ALU.is_gt, op1=ALU.add), s_V)
                mark(f"xnq_{s}_{m}", s_V)
                if s >= 2:
                    w(gp, s_xnT[b], f"xnT_{s - 2}_{m}")
                w(gp, s_S, f"xnu_{s}_{m}")
                inc(gp.dma_start(out=xn_scr[m].rearrange("k t c -> t k c"),
                                 in_=xnb[:, b, :].rearrange("t (k c) -> t k c", k=KD)),
                    s_xnw[b], 16)
                mark(f"xnw_{s}_{m}", s_xnw[b])
                if s >= 2:
                    w(gp, s_xqT[_fpar(s - 2, m)], f"xqT_{s - 2}_{m}")
                w(gp, s_V, f"xnq_{s}_{m}")
                inc(gp.dma_start(out=xnq_scr[ss, m].rearrange("k t c -> t k c"),
                                 in_=xnqb[:, b, :].rearrange("t (k c) -> t k c", k=KD)),
                    s_xqw[b], 16)
                mark(f"xqw_{s}_{m}", s_xqw[b])

            def emit_A1c(s, m):
                b = state["a1_b"][(s, m)]
                seq = s * TC + m
                prev = state["c_prev"]
                state["c_prev"] = (s, m)
                prev2 = divmod(seq - 2, TC) if seq >= 2 else None
                w(sync, s_xnw[b], f"xnw_{s}_{m}")
                if prev2 is not None:
                    w(sync, s_T, f"proj_{prev2[0]}_{prev2[1]}")
                inc(sync.dma_start_transpose(out=xnTb[:, b, :],
                                             in_=xn_scr[m].rearrange("k t c -> (k t) c")),
                    s_xnT[b], 16)
                mark(f"xnT_{s}_{m}", s_xnT[b])
                w(ten, s_xnT[b], f"xnT_{s}_{m}")
                if seq == 0:
                    w(ten, s_V, "qk_bf")
                if prev is not None:
                    w(ten, s_S, f"expqk_{prev[0]}_{prev[1]}")
                for kd in range(KD):
                    xt = xnTb[:, b, kd * 128:(kd + 1) * 128]
                    ten.matmul(ppq[:, 0:E], xt, qkb[:, 0, kd, :],
                               start=(kd == 0), stop=(kd == KD - 1))
                    mmk = ten.matmul(ppk[:, 0:E], xt, qkb[:, 1, kd, :],
                                     start=(kd == 0), stop=(kd == KD - 1))
                inc(mmk, s_T)
                mark(f"proj_{s}_{m}", s_T)

            def emit_A1bS(s, m):
                b = state["a1_b"][(s, m)]
                seq = s * TC + m
                prev2 = divmod(seq - 2, TC) if seq >= 2 else None
                w(sca, s_T, f"proj_{s}_{m}")
                if prev2 is not None:
                    w(sca, s_qw[b], f"qw_{prev2[0]}_{prev2[1]}")
                sca.activation(out=qeb[:, b, :], in_=ppq[:, 0:E], func=AF.Exp,
                               bias=ccst[:, 0:1], scale=1.0)
                sca.activation(out=keb[:, b, :], in_=ppk[:, 0:E], func=AF.Exp,
                               bias=ccst[:, 0:1], scale=1.0)
                inc(sca.drain(), s_S)
                mark(f"expqk_{s}_{m}", s_S)

            def emit_A1bT(s, m):
                b = state["a1_b"].pop((s, m))
                w(ten, s_S, f"expqk_{s}_{m}")
                if m == 0 and s >= 1:
                    w(ten, s_V, f"kvev_{s - 1}")
                    w(ten, s_V, "wscols")
                for ec in range(EC):
                    ke = keb[:, b, ec * 128:(ec + 1) * 128]
                    for half in range(2):
                        mmk = ten.matmul(pkv[:, ec, half * 512:(half + 1) * 512], ke,
                                         xnb[:, b, half * 512:(half + 1) * 512],
                                         start=(m == 0), stop=(m == TC - 1))
                inc(mmk, s_T)
                mark(f"kv_{s}_{m}", s_T)
                w(gp, s_S, f"expqk_{s}_{m}")
                inc(gp.dma_start(out=q_scr[s, m].rearrange("k t c -> t k c"),
                                 in_=qeb[:, b, :].rearrange("t (k c) -> t k c", k=EC)),
                    s_qw[b], 16)
                mark(f"qw_{s}_{m}", s_qw[b])

            def emit_kvev(s):
                w(vec, s_T, f"kv_{s}_{TC - 1}")
                for ec in range(EC):
                    vec.tensor_copy(kvb[:, s, ec, 0:1024], pkv[:, ec, :])
                vec.memset(kvb[:, s, 0, 1024:1025], 1.0)
                vec.memset(kvb[:, s, 1, 1024:1025], 1.0)
                inc(vec.drain(), s_V)
                mark(f"kvev_{s}", s_V)

            def tail_vp(ts, tm, tfb):
                c_su = 40 + 3 * tfb
                w(sca, s_V, f"hcols_{ts}_{tm}")
                sca.activation(out=hb[:, tfb, :], in_=hb[:, tfb, :], func=AF.Identity,
                               bias=ccst[:, 1:2], scale=col(c_su))
                inc(sca.drain(), s_S)
                mark(f"vp_{ts}_{tm}", s_S)

            def tail_cd(ts, tm, tfb):
                w(vec, s_S, f"vp_{ts}_{tm}")
                vec.drain()
                inc(vec.tensor_scalar(out=chb[:], in0=hb[:, tfb, :], scalar1=C32,
                                      scalar2=C32 + 1.0, op0=ALU.add, op1=ALU.subtract), s_V)
                mark(f"hc_{ts}_{tm}", s_V)
                vec.drain()
                inc(vec.tensor_tensor(out=dbuf[:, 0:H // 2], in0=hb[:, tfb, 0:H // 2],
                                      in1=chb[:, 0:H // 2], op=ALU.subtract), s_V)
                mark(f"hd_{ts}_{tm}", s_V)
                tprev = state.get("t_prev")
                if tprev is not None:
                    w(gp, s_V, f"hq_{tprev[0]}_{tprev[1]}")
                w(gp, s_V, f"hc_{ts}_{tm}")
                inc(gp.tensor_tensor(out=dbuf[:, H // 2:H], in0=hb[:, tfb, H // 2:H],
                                     in1=chb[:, H // 2:H], op=ALU.subtract), s_G)
                mark(f"hg_{ts}_{tm}", s_G)

            def tail_V2(ts, tm, tfb):
                w(vec, s_G, f"hg_{ts}_{tm}")
                tprev = state.get("t_prev")
                if tprev is not None:
                    w(vec, s_hqw[tprev[2]], f"hqw_{tprev[0]}_{tprev[1]}")
                vec.drain()
                inc(vec.scalar_tensor_tensor(out=hqb[:], in0=dbuf[:], scalar=1.0,
                                             in1=chb[:],
                                             op0=ALU.is_gt, op1=ALU.add), s_V)
                mark(f"hq_{ts}_{tm}", s_V)
                w(gp, s_V, f"hq_{ts}_{tm}")
                hqv = hqb.rearrange("t (k c) -> t k c", k=KH)
                gp.dma_start(out=hq_scr[ts, tm, 0:KH // 2].rearrange("k t c -> t k c"),
                             in_=hqv[:, 0:KH // 2, :]).then_inc(s_hqw[tfb].h, 16)
                s_hqw[tfb].c += 16
                inc(gp.dma_start(out=hq_scr[ts, tm, KH // 2:KH].rearrange("k t c -> t k c"),
                                 in_=hqv[:, KH // 2:KH, :]),
                    s_hqw[tfb], 16)
                mark(f"hqw_{ts}_{tm}", s_hqw[tfb])
                state["t_prev"] = (ts, tm, tfb)

            def flush_tail():
                ft = state.get("f_tail")
                if ft is not None:
                    tail_vp(*ft)
                    tail_cd(*ft)
                    tail_V2(*ft)
                    state["f_tail"] = None

            def emit_FFN1(s, m):
                seq = s * TC + m
                ss = s % 2
                forder = state["f_prev"]
                g = len(forder)
                fb = g % 2
                fprev = forder[-1] if forder else None
                fprev2 = forder[-2] if len(forder) >= 2 else None
                forder.append((s, m))
                w(sync, s_xqw[m % 2], f"xqw_{s}_{m}")
                if fprev2 is not None:
                    w(sync, s_T, f"f1_{fprev2[0]}_{fprev2[1]}_{NH - 1}")
                if g == 0:
                    w(sync, s_wT, "wq1T")
                inc(sync.dma_start_transpose(out=xqTb[:, fb, :],
                                             in_=xnq_scr[ss, m].rearrange("k t c -> (k t) c")),
                    s_xqT[fb], 16)
                mark(f"xqT_{s}_{m}", s_xqT[fb])
                w(ten, s_xqT[fb], f"xqT_{s}_{m}")
                if g == 0:
                    w(ten, s_wT, "wq1T")
                wq1v = wqT.rearrange("p k h -> p (k h)")
                for n in range(NH):
                    if n == 4 and state.get("f_tail") is not None:
                        tail_vp(*state["f_tail"])
                    if n >= 2:
                        w(ten, s_S, f"g1_{s}_{m}_{n - 2}")
                    elif fprev is not None:
                        w(ten, s_S, f"g1_{fprev[0]}_{fprev[1]}_{NH - 2 + n}")
                    for kd in range(KD):
                        mmk = ten.matmul(pf1[:, n % 2, :],
                                         xqTb[:, fb, kd * 128:(kd + 1) * 128],
                                         wq1v[:, kd * H + n * 512: kd * H + (n + 1) * 512],
                                         start=(kd == 0), stop=(kd == KD - 1))
                    inc(mmk, s_T)
                    mark(f"f1_{s}_{m}_{n}", s_T)
                    w(sca, s_T, f"f1_{s}_{m}_{n}")
                    if n == 0 and fprev2 is not None:
                        w(sca, s_V, f"hd_{fprev2[0]}_{fprev2[1]}")
                        w(sca, s_G, f"hg_{fprev2[0]}_{fprev2[1]}")
                    inc(sca.activation(out=hb[:, fb, n * 512:(n + 1) * 512], in_=pf1[:, n % 2, :],
                                       func=AF.Gelu, bias=ccst[:, 0:1],
                                       scale=acolT[:, seq:seq + 1]), s_S)
                    mark(f"g1_{s}_{m}_{n}", s_S)
                ft = state.get("f_tail")
                if ft is not None:
                    tail_cd(*ft)
                w(vec, s_S, f"g1_{s}_{m}_{NH - 1}")
                if fprev2 is not None:
                    w(vec, s_S, f"vp_{fprev2[0]}_{fprev2[1]}")
                c_mx, c_inv, c_su = 38 + 3 * fb, 39 + 3 * fb, 40 + 3 * fb
                vec.reduce_max(col(c_mx), hb[:, fb, :], axis=AX.X, apply_absolute_value=True)
                vec.drain()
                vec.tensor_scalar(out=col(c_mx), in0=col(c_mx), scalar1=EPS_Q, scalar2=None, op0=ALU.max)
                vec.drain()
                vec.reciprocal(col(c_inv), col(c_mx))
                vec.drain()
                vec.tensor_scalar(out=col(c_su), in0=col(c_inv), scalar1=-127.0, scalar2=None, op0=ALU.mult)
                vec.tensor_scalar(out=bcolT[:, seq:seq + 1], in0=col(c_mx),
                                  scalar1=col(C_S2), scalar2=-1.0 / 254.0, op0=ALU.mult, op1=ALU.mult)
                inc(vec.drain(), s_V)
                mark(f"hcols_{s}_{m}", s_V)
                if ft is not None:
                    tail_V2(*ft)
                state["f_tail"] = (s, m, fb)

            def emit_w1p1(c):
                b = c % 2
                if c >= 2:
                    w(sync, s_V, f"wred_0_{c - 2}")
                inc(sync.dma_start(out=wb[:, b, :], in_=w1_d[c * 128:(c + 1) * 128, :]),
                    s_w[b], 16)
                mark(f"wld_0_{c}", s_w[b])
                w(vec, s_w[b], f"wld_0_{c}")
                inc(vec.reduce_sum(wcols[:, c:c + 1], wb[:, b, :], axis=AX.X,
                                   apply_absolute_value=True), s_V)
                mark(f"wred_0_{c}", s_V)

            def emit_w2p1(c):
                if c >= 1:
                    w(sync, s_V, f"wred_1_{c - 1}")
                else:
                    w(sync, s_V, "qk_bf")
                inc(sync.dma_start(out=wb2, in_=w2_d[c * 128:(c + 1) * 128, :]), s_wb2a, 16)
                mark(f"wld_1_{c}", s_wb2a)
                w(vec, s_wb2a, f"wld_1_{c}")
                inc(vec.reduce_sum(wcols[:, 32 + c:33 + c], wb2, axis=AX.X,
                                   apply_absolute_value=True), s_V)
                mark(f"wred_1_{c}", s_V)

            def emit_scalc():
                w(vec, s_V, f"wred_0_{W1C - 1}")
                w(vec, s_V, f"wred_1_{W2C - 1}")
                vec.reduce_sum(col(C_S1), wcols[:, 0:W1C], axis=AX.X)
                vec.reduce_sum(col(C_S2), wcols[:, 32:32 + W2C], axis=AX.X)
                inc(vec.drain(), s_V)
                mark("wcolsum", s_V)
                w(ten, s_V, "wcolsum")
                w(ten, s_V, "kvev_0")
                ten.matmul(pkv[0:1, 1, 0:1], onescol[:], col(C_S1), start=True, stop=True)
                inc(ten.matmul(pkv[0:1, 1, 1:2], onescol[:], col(C_S2), start=True, stop=True), s_T)
                mark("wsred", s_T)
                w(vec, s_T, "wsred")
                vec.tensor_copy(sval[:, 0:2], pkv[0:1, 1, 0:2])
                vec.drain()
                vec.tensor_scalar(out=sval[:, 2:3], in0=sval[:, 0:1], scalar1=0.5 / (H * D),
                                  scalar2=None, op0=ALU.mult)
                vec.tensor_scalar(out=sval[:, 3:4], in0=sval[:, 1:2], scalar1=0.5 / (D * H),
                                  scalar2=None, op0=ALU.mult)
                inc(vec.drain(), s_V)
                mark("wsval", s_V)
                w(ten, s_V, "wsval")
                inc(ten.matmul(pkv[:, 1, 4:6], onesrow[:], sval[:, 2:4], start=True, stop=True), s_T)
                mark("wsbcast", s_T)
                w(vec, s_T, "wsbcast")
                vec.tensor_copy(cols[:, C_TH1P:C_TH1P + 2], pkv[:, 1, 4:6])
                vec.drain()
                vec.tensor_scalar(out=col(C_S1), in0=col(C_TH1P), scalar1=2.0, scalar2=None, op0=ALU.mult)
                vec.tensor_scalar(out=col(C_S2), in0=col(C_TH2P), scalar1=2.0, scalar2=None, op0=ALU.mult)
                vec.tensor_scalar(out=col(C_TH1N), in0=col(C_TH1P), scalar1=-1.0, scalar2=None, op0=ALU.mult)
                vec.tensor_scalar(out=col(C_TH2N), in0=col(C_TH2P), scalar1=-1.0, scalar2=None, op0=ALU.mult)
                vec.drain()
                vec.tensor_scalar(out=acolT[:, 0:TC], in0=statc[:, 0, 1, :],
                                  scalar1=col(C_S1), scalar2=-1.0 / 254.0, op0=ALU.mult, op1=ALU.mult)
                if n_seq > 1:
                    vec.tensor_scalar(out=acolT[:, TC:2 * TC], in0=statc[:, 1, 1, :],
                                      scalar1=col(C_S1), scalar2=-1.0 / 254.0, op0=ALU.mult, op1=ALU.mult)
                inc(vec.drain(), s_V)
                mark("wscols", s_V)

            def emit_wp2(wi, c):
                b = c % 2
                wd = D if wi == 0 else H
                thp = col(C_TH1P) if wi == 0 else col(C_TH2P)
                thn = col(C_TH1N) if wi == 0 else col(C_TH2N)
                s_q = s_w1q if wi == 0 else s_w2q
                scr = wq1_scr if wi == 0 else wq2_scr
                if wi == 1:
                    if c >= 1:
                        if c % 2 == 0:
                            w(sync, s_V, f"wsgV_1_{c - 1}")
                        else:
                            w(sync, s_S, f"wsgn_1_{c - 1}")
                    else:
                        w(sync, s_V, f"wred_1_{W2C - 1}")
                    inc(sync.dma_start(out=wb2, in_=w2_d[c * 128:(c + 1) * 128, :]), s_wb2b, 16)
                else:
                    w(sync, s_S, f"wsgn_0_{c - 2}" if c >= 2 else f"wsgn_1_{W2C - 1}")
                    inc(sync.dma_start(out=wb[:, b, :], in_=w1_d[c * 128:(c + 1) * 128, :]),
                        s_wp2[b], 16)
                mark(f"wld2_{wi}_{c}", s_wb2b if wi == 1 else s_wp2[b])
                vroute = (wi == 1 and c % 2 == 1)
                if not vroute:
                    w(sca, s_wb2b if wi == 1 else s_wp2[b], f"wld2_{wi}_{c}")
                    if c == 0 and wi == 1:
                        w(sca, s_V, "wscols")
                    if c >= 1:
                        w(sca, s_V, f"wtrn_{wi}_{c - 1}")
                    elif wi == 0:
                        w(sca, s_V, f"wtrn_1_{W2C - 1}")
                    src = wb[:, b, :] if wi == 0 else wb2
                    sca.activation(out=wsg[:, 0, 0:wd], in_=src, func=AF.Sign, bias=thn, scale=1.0)
                    sca.activation(out=wsg[:, 1, 0:wd], in_=src, func=AF.Sign, bias=thp, scale=1.0)
                    inc(sca.drain(), s_S)
                    mark(f"wsgn_{wi}_{c}", s_S)
                    w(vec, s_S, f"wsgn_{wi}_{c}")
                else:
                    w(vec, s_wb2b, f"wld2_{wi}_{c}")
                    vec.drain()
                    vec.tensor_scalar(out=wsg[:, 0, 0:wd], in0=wb2, scalar1=thp,
                                      scalar2=2.0, op0=ALU.is_ge, op1=ALU.mult)
                    vec.tensor_scalar(out=wsg[:, 1, 0:wd], in0=wb2, scalar1=thn,
                                      scalar2=-2.0, op0=ALU.is_lt, op1=ALU.mult)
                    inc(vec.drain(), s_V)
                    mark(f"wsgV_{wi}_{c}", s_V)
                    mark(f"wsgn_{wi}_{c}", s_S)
                if c >= 2:
                    w(vec, s_q[b], f"wqw_{wi}_{c - 2}")
                elif wi == 0:
                    w(vec, s_w2q[(W2C - 2 + c) % 2], f"wqw_1_{W2C - 2 + c}")
                inc(vec.tensor_tensor(out=wqo[:, b, 0:wd], in0=wsg[:, 0, 0:wd],
                                      in1=wsg[:, 1, 0:wd], op=ALU.add), s_V)
                mark(f"wtrn_{wi}_{c}", s_V)
                w(gp, s_V, f"wtrn_{wi}_{c}")
                inc(gp.dma_start(out=scr[c * 128:(c + 1) * 128, 0:wd], in_=wqo[:, b, 0:wd]),
                    s_q[b], 16)
                mark(f"wqw_{wi}_{c}", s_q[b])

            def emit_wq1T():
                w(sync, s_w1q[0], f"wqw_0_{W1C - 2}")
                w(sync, s_w1q[1], f"wqw_0_{W1C - 1}")
                w(sync, s_w2q[0], f"wqw_1_{W2C - 2}")
                w(sync, s_w2q[1], f"wqw_1_{W2C - 1}")
                wq1v = wqT.rearrange("p k h -> p (k h)")
                for kd in range(KD):
                    inc(sync.dma_start_transpose(out=wq1v[:, kd * H:(kd + 1) * H],
                                                 in_=wq1_scr[:, kd * 128:(kd + 1) * 128]),
                        s_wT, 16)
                mark("wq1T", s_wT)

            def emit_P2(s, m):
                seq = s * TC + m
                b = m % 2
                prev = divmod(seq - 1, TC) if seq >= 1 else None
                prev2 = divmod(seq - 2, TC) if seq >= 2 else None
                if prev2 is not None:
                    w(sync, s_T, f"f2_{prev2[0]}_{prev2[1]}")
                    w(sync, s_V, f"ep2_{prev2[0]}_{prev2[1]}")
                if seq == 0:
                    w(sync, s_V, f"hq_{n_seq - 1}_{TC - 1}")
                    w(sync, s_G, f"hg_{n_seq - 1}_{TC - 1}")
                    w(sync, s_S, f"vp_{n_seq - 1}_{TC - 1}")
                w(sync, s_hqw[_fpar(s, m)], f"hqw_{s}_{m}")
                inc(sync.dma_start_transpose(out=hqTb[:, b, :],
                                             in_=hq_scr[s, m].rearrange("k t c -> (k t) c")),
                    s_hqT[b], 16)
                mark(f"hqT_{s}_{m}", s_hqT[b])
                w(sync, s_qw[b], f"qw_{s}_{m}")
                inc(sync.dma_start_transpose(out=qTb[:, b, :],
                                             in_=q_scr[s, m].rearrange("k t c -> (k t) c")),
                    s_qT[b], 16)
                mark(f"qT_{s}_{m}", s_qT[b])
                xb_load(s_x2, "P2", s, m, x_d[s, m * 128:(m + 1) * 128, :])
                mark(f"x2_{s}_{m}", s_x2[b])
                pb = seq % 2
                pnum0 = ppq[:] if pb == 0 else pf1[:, 0, :]
                pnum1 = ppk[:] if pb == 0 else pf1[:, 1, :]
                pz = pkv[:, 1, 8:9] if pb == 0 else pkv[:, 1, 520:521]
                w(ten, s_qT[b], f"qT_{s}_{m}")
                if seq == 0:
                    w(ten, s_V, f"kvev_{n_seq - 1}")
                if prev2 is not None:
                    w(ten, s_V, f"ep1_{prev2[0]}_{prev2[1]}")
                if seq == 1:
                    sl, sm = state["f_prev"][-1]
                    w(ten, s_S, f"g1_{sl}_{sm}_{NH - 1}")
                if pb == 1 and prev is not None:
                    w(ten, s_V, f"ep2_{prev[0]}_{prev[1]}")
                for ec in range(EC):
                    qt = qTb[:, b, ec * 128:(ec + 1) * 128]
                    ten.matmul(pnum0, qt, kvb[:, s, ec, 0:512],
                               start=(ec == 0), stop=(ec == EC - 1))
                    ten.matmul(pnum1, qt, kvb[:, s, ec, 512:1024],
                               start=(ec == 0), stop=(ec == EC - 1))
                    mmk = ten.matmul(pz, qt, kvb[:, s, ec, 1024:1025],
                                     start=(ec == 0), stop=(ec == EC - 1))
                inc(mmk, s_T)
                mark(f"num_{s}_{m}", s_T)
                w(ten, s_hqT[b], f"hqT_{s}_{m}")
                if seq == 0:
                    sl, sm = state["f_prev"][-1]
                    w(ten, s_S, f"g1_{sl}_{sm}_{NH - 1}")
                if prev2 is not None:
                    w(ten, s_V, f"ep2_{prev2[0]}_{prev2[1]}")
                if pb == 0 and prev is not None:
                    w(ten, s_V, f"ep1_{prev[0]}_{prev[1]}")
                pf2 = [pf1[:, 0, :], pf1[:, 1, :]] if (pb == 0) else \
                      [pkv[:, 0, 0:512], pkv[:, 0, 512:1024]]
                for kh in range(KH):
                    if seq == 0:
                        w(ten, s_wT, f"wq2T_{kh}")
                    ht = hqTb[:, b, kh * 128:(kh + 1) * 128]
                    for half in range(2):
                        mmk = ten.matmul(pf2[half], ht, wqT[:, kh, half * 512:(half + 1) * 512],
                                         start=(kh == 0), stop=(kh == KH - 1))
                inc(mmk, s_T)
                mark(f"f2_{s}_{m}", s_T)
                w(vec, s_T, f"num_{s}_{m}")
                w(vec, s_x2[b], f"x2_{s}_{m}")
                if prev is not None:
                    w(vec, s_st[prev[1] % 2], f"st_{prev[0]}_{prev[1]}")
                zr = zrb[:, pb:pb + 1]
                vec.drain()
                vec.tensor_scalar(out=zr, in0=pz, scalar1=EPS_Z,
                                  scalar2=None, op0=ALU.add)
                vec.drain()
                vec.reciprocal(zr, zr)
                vec.drain()
                vec.scalar_tensor_tensor(out=ob[:, 0:512], in0=pnum0, scalar=zr,
                                         in1=xb[:, b, 0:512], op0=ALU.mult, op1=ALU.add)
                inc(vec.scalar_tensor_tensor(out=ob[:, 512:1024], in0=pnum1, scalar=zr,
                                             in1=xb[:, b, 512:1024], op0=ALU.mult, op1=ALU.add), s_V)
                mark(f"ep1_{s}_{m}", s_V)
                w(vec, s_T, f"f2_{s}_{m}")
                bc = bcolT[:, seq:seq + 1]
                vec.drain()
                vec.scalar_tensor_tensor(out=ob[:, 0:512], in0=pf2[0], scalar=bc,
                                         in1=ob[:, 0:512], op0=ALU.mult, op1=ALU.add)
                inc(vec.scalar_tensor_tensor(out=ob[:, 512:1024], in0=pf2[1], scalar=bc,
                                             in1=ob[:, 512:1024], op0=ALU.mult, op1=ALU.add), s_V)
                mark(f"ep2_{s}_{m}", s_V)
                w(gp, s_V, f"ep2_{s}_{m}")
                inc(gp.dma_start(out=out_d[s, m * 128:(m + 1) * 128, :], in_=ob[:]),
                    s_st[b], 16)
                mark(f"st_{s}_{m}", s_st[b])

            vec.memset(ccst[:, 0:1], 0.0)
            vec.memset(ccst[:, 1:2], 0.5)
            vec.memset(ccst[:, 2:3], 1.0)
            vec.memset(ccst[:, 3:4], -0.5)
            vec.memset(onescol[:], 1.0)
            vec.memset(onesrow[:], 1.0)
            inc(vec.drain(), s_V)
            mark("consts", s_V)
            w(sca, s_V, "consts")
            w(ten, s_V, "consts")
            w(gp, s_V, "consts")
            inc(sync.dma_start(out=wb2.rearrange("t (k e) -> t k e", k=KD)[:, :, 0:E],
                               in_=qp_d.rearrange("(k t) e -> t k e", k=KD)), s_qk, 16)
            inc(sync.dma_start(out=wb2.rearrange("t (k e) -> t k e", k=KD)[:, :, E:2 * E],
                               in_=kp_d.rearrange("(k t) e -> t k e", k=KD)), s_qk, 16)
            mark("qk_ld", s_qk)
            w(vec, s_qk, "qk_ld")
            vec.tensor_copy(qkb[:, 0, :, :], wb2.rearrange("t (k e) -> t k e", k=KD)[:, :, 0:E])
            vec.tensor_copy(qkb[:, 1, :, :], wb2.rearrange("t (k e) -> t k e", k=KD)[:, :, E:2 * E])
            inc(vec.drain(), s_V)
            mark("qk_bf", s_V)

            def runjob(j):
                kind = j[0]
                if kind == "w1p1":
                    emit_w1p1(j[1])
                elif kind == "w2p1":
                    emit_w2p1(j[1])
                elif kind == "wp2":
                    emit_wp2(j[1], j[2])
                elif kind == "A0":
                    emit_A0(j[1], j[2])
                elif kind == "A1a":
                    emit_A1a(j[1], j[2])
                elif kind == "A1c":
                    emit_A1c(j[1], j[2])
                elif kind == "A1b":
                    emit_A1bS(j[1], j[2])
                    emit_A1bT(j[1], j[2])
                elif kind == "batch":
                    emit_batch(j[1])

            wjobs = [("w1p1", c) for c in range(W1C)] + [("w2p1", c) for c in range(W2C)]
            ajobs = [("A0", 0, m) for m in range(TC)] + [("batch", 0)]
            for m in range(TC):
                if m >= 2:
                    ajobs.append(("A1b", 0, m - 2))
                ajobs.append(("A1a", 0, m))
                if m >= 1:
                    ajobs.append(("A1c", 0, m - 1))
                ajobs.append(("A0", 1, m))
            ajobs += [("A1b", 0, TC - 2), ("A1c", 0, TC - 1),
                      ("A1b", 0, TC - 1), ("batch", 1)]
            wi_ = ai_ = 0
            while wi_ < len(wjobs) or ai_ < len(ajobs):
                if ai_ < len(ajobs):
                    runjob(ajobs[ai_]); ai_ += 1
                if wi_ < len(wjobs):
                    runjob(wjobs[wi_]); wi_ += 1
            emit_kvev(0)
            emit_scalc()
            wjobs = [("wp2", 1, c) for c in range(W2C)] + [("wp2", 0, c) for c in range(W1C)]
            ajobs = []
            for m in range(TC):
                if m >= 2:
                    ajobs.append(("A1b", 1, m - 2))
                ajobs.append(("A1a", 1, m))
                if m >= 1:
                    ajobs.append(("A1c", 1, m - 1))
                ajobs += [("A0", 2, m), ("A0", 3, m)]
            ajobs += [("A1b", 1, TC - 2), ("A1c", 1, TC - 1),
                      ("A1b", 1, TC - 1), ("batch", 2), ("batch", 3)]
            wi_ = ai_ = 0
            while wi_ < len(wjobs) or ai_ < len(ajobs):
                if ai_ < len(ajobs):
                    runjob(ajobs[ai_]); ai_ += 1
                if wi_ < len(wjobs):
                    runjob(wjobs[wi_]); wi_ += 1
            emit_kvev(1)
            emit_wq1T()
            for m in range(TC):
                emit_FFN1(0, m)
                emit_FFN1(1, m)
            NK = 2 * TC
            a1map = lambda k: (2 + k // TC, k % TC)
            emit_A1a(*a1map(0))
            if NK > 1:
                emit_A1a(*a1map(1))
            emit_A1c(*a1map(0))
            emit_A1bS(*a1map(0))
            emit_A1bT(*a1map(0))
            if NK > 2:
                emit_A1a(*a1map(2))
            if NK > 1:
                emit_A1c(*a1map(1))
            for k in range(NK):
                if k + 1 < NK:
                    emit_A1bS(*a1map(k + 1))
                emit_FFN1(*a1map(k))
                if k + 1 < NK:
                    emit_A1bT(*a1map(k + 1))
                    if (k + 1) % TC == TC - 1:
                        emit_kvev(2 + (k + 1) // TC)
                if k + 2 < NK:
                    emit_A1c(*a1map(k + 2))
                if k + 3 < NK:
                    emit_A1a(*a1map(k + 3))
            flush_tail()
            w(sync, s_T, f"f1_3_{TC - 1}_{NH - 1}")
            for kh in range(KH):
                inc(sync.dma_start_transpose(out=wqT[:, kh, :],
                                             in_=wq2_scr[:, kh * 128:(kh + 1) * 128]),
                    s_wT, 16)
                mark(f"wq2T_{kh}", s_wT)
            mark("wq2T", s_wT)
            for s in range(n_seq):
                for m in range(TC):
                    emit_P2(s, m)
            w(sync, s_st[0], f"st_{n_seq - 1}_{TC - 2}")
            w(sync, s_st[1], f"st_{n_seq - 1}_{TC - 1}")

        fake = _Proxy(None)

        @blk.sync
        def _(sync):
            run_schedule(_Proxy(sync), fake, fake, fake, fake)

        @blk.vector
        def _(vector):
            run_schedule(fake, _Proxy(vector), fake, fake, fake)

        @blk.scalar
        def _(scalar):
            run_schedule(fake, fake, _Proxy(scalar), fake, fake)

        @blk.tensor
        def _(tensor):
            run_schedule(fake, fake, fake, _Proxy(tensor), fake)

        @blk.gpsimd
        def _(gpsimd):
            run_schedule(fake, fake, fake, fake, _Proxy(gpsimd))

    return nc


_NC_CACHE = {}


def _get_nc(n_seq, T):
    key = (n_seq, T)
    if key not in _NC_CACHE:
        _NC_CACHE[key] = build_nc(n_seq, T)
    return _NC_CACHE[key]


def kernel(L0_attn_x, L0_attn_q_proj, L0_attn_k_proj, L0_ffn_w1, L0_ffn_w2):
    x = np.ascontiguousarray(np.asarray(L0_attn_x, dtype=np.float32))
    qp = np.ascontiguousarray(np.asarray(L0_attn_q_proj, dtype=np.float32))
    kp = np.ascontiguousarray(np.asarray(L0_attn_k_proj, dtype=np.float32))
    w1 = np.ascontiguousarray(np.asarray(L0_ffn_w1, dtype=np.float32))
    w2 = np.ascontiguousarray(np.asarray(L0_ffn_w2, dtype=np.float32))
    B, T, d = x.shape
    n_seq = B // N_CORES
    nc = _get_nc(n_seq, T)
    in_maps = []
    for c in range(N_CORES):
        in_maps.append({
            "x": x[c * n_seq:(c + 1) * n_seq],
            "q_proj": qp, "k_proj": kp, "w1": w1, "w2": w2,
        })
    res = run_bass_kernel_spmd(nc, in_maps, core_ids=list(range(N_CORES)))
    return np.concatenate([np.asarray(r["out"]) for r in res.results], axis=0)


# version 29
# speedup vs baseline: 1.0042x; 1.0042x over previous
import numpy as np
import concourse.bass as bass
import concourse.mybir as mybir
from concourse.alu_op_type import AluOpType as ALU
from concourse.bass_utils import run_bass_kernel_spmd
from contextlib import ExitStack

F32 = mybir.dt.float32
BF16 = mybir.dt.bfloat16
I16 = mybir.dt.int16
AF = mybir.ActivationFunctionType
AX = mybir.AxisListType

C32 = 12582912.0
D, E, H = 1024, 256, 4096
KD, EC, KH, NH = 8, 2, 32, 8
W1C, W2C = 32, 8
EPS_LN = 1e-5
EPS_Q = 1e-5
EPS_Z = 1e-8

N_CORES = 8

C_S1, C_S2, C_TH1P, C_TH2P, C_TH1N, C_TH2N = 32, 33, 34, 35, 36, 37


class _FakeInstr:
    def then_inc(self, *a, **k):
        return self


_FAKE = _FakeInstr()


class _Proxy:
    def __init__(self, eng):
        self._e = eng

    def __getattr__(self, name):
        if self._e is None:
            return lambda *a, **k: _FAKE
        return getattr(self._e, name)


class Sem:
    def __init__(self, handle):
        self.h = handle
        self.c = 0


def build_nc(n_seq, T):
    TC = T // 128
    assert TC % 2 == 0
    NCH = n_seq * TC
    nc = bass.Bass("TRN2", target_bir_lowering=False)

    x_d = nc.dram_tensor("x", [n_seq, T, D], F32, kind="ExternalInput").ap()
    qp_d = nc.dram_tensor("q_proj", [D, E], F32, kind="ExternalInput").ap()
    kp_d = nc.dram_tensor("k_proj", [D, E], F32, kind="ExternalInput").ap()
    w1_d = nc.dram_tensor("w1", [H, D], F32, kind="ExternalInput").ap()
    w2_d = nc.dram_tensor("w2", [D, H], F32, kind="ExternalInput").ap()
    out_d = nc.dram_tensor("out", [n_seq, T, D], F32, kind="ExternalOutput").ap()

    xn_scr = nc.dram_tensor("xn_scr", [TC, KD, 128, 128], BF16).ap()
    xnq_scr = nc.dram_tensor("xnq_scr", [2, TC, KD, 128, 128], BF16).ap()
    q_scr = nc.dram_tensor("q_scr", [n_seq, TC, EC, 128, 128], BF16).ap()
    hq_scr = nc.dram_tensor("hq_scr", [n_seq, TC, KH, 128, 128], BF16).ap()
    wq1_scr = nc.dram_tensor("wq1_scr", [H, D], BF16).ap()
    wq2_scr = nc.dram_tensor("wq2_scr", [D, H], BF16).ap()

    st = ExitStack()
    EE = st.enter_context
    with st:
        wqT = EE(nc.sbuf_tensor("wqT", [128, KH, 1024], BF16))
        qkb = EE(nc.sbuf_tensor("qkb", [128, 2, KD, E], BF16))
        kvb = EE(nc.sbuf_tensor("kvb", [128, n_seq, EC, 1025], BF16))
        xb = EE(nc.sbuf_tensor("xb", [128, 2, D], F32))
        xa0 = EE(nc.sbuf_tensor("xa0", [128, 2, D], F32))
        xnb = EE(nc.sbuf_tensor("xnb", [128, 2, D], BF16))
        xnTb = EE(nc.sbuf_tensor("xnTb", [128, 2, D], BF16))
        xqTb = EE(nc.sbuf_tensor("xqTb", [128, 2, D], BF16))
        cb = EE(nc.sbuf_tensor("cb", [128, D], I16))
        dbuf = EE(nc.sbuf_tensor("dbuf", [128, H], BF16))
        xnqb = EE(nc.sbuf_tensor("xnqb", [128, 2, D], BF16))
        qeb = EE(nc.sbuf_tensor("qeb", [128, 2, E], BF16))
        keb = EE(nc.sbuf_tensor("keb", [128, 2, E], BF16))
        qTb = EE(nc.sbuf_tensor("qTb", [128, 2, E], BF16))
        ob = EE(nc.sbuf_tensor("ob", [128, D], F32))
        cols = EE(nc.sbuf_tensor("cols", [128, 64], F32))
        statc = EE(nc.sbuf_tensor("statc", [128, 2, 4, TC], F32))
        dcols = EE(nc.sbuf_tensor("dcols", [128, 2, 4, TC], F32))
        ccst = EE(nc.sbuf_tensor("ccst", [128, 4], F32))
        acolT = EE(nc.sbuf_tensor("acolT", [128, NCH], F32))
        bcolT = EE(nc.sbuf_tensor("bcolT", [128, NCH], F32))
        zrb = EE(nc.sbuf_tensor("zrb", [128, 2], F32))
        wcols = EE(nc.sbuf_tensor("wcols", [128, 40], F32))
        onescol = EE(nc.sbuf_tensor("onescol", [128, 1], F32))
        onesrow = EE(nc.sbuf_tensor("onesrow", [1, 128], F32))
        sval = EE(nc.sbuf_tensor("sval", [1, 4], F32))

        ARENA = 57344
        ph = EE(nc.sbuf_tensor("arena", [128, ARENA], mybir.dt.uint8))
        base = nc.lookup_mloc(ph).addr
        wb_t = nc.alloc_sbuf_tensor_at("wb", [128, 2, D], F32, offset=base)
        wb2_t = nc.alloc_sbuf_tensor_at("wb2", [128, H], F32, offset=base + 8192)
        wsg_t = nc.alloc_sbuf_tensor_at("wsg", [128, 2, H], BF16, offset=base + 24576)
        wqo_t = nc.alloc_sbuf_tensor_at("wqo", [128, 2, H], BF16, offset=base + 40960)
        wb, wb2, wsg, wqo = wb_t.ap(), wb2_t.ap(), wsg_t.ap(), wqo_t.ap()
        hb_t = nc.alloc_sbuf_tensor_at("hb", [128, 2, H], F32, offset=base)
        hqTb_t = nc.alloc_sbuf_tensor_at("hqTb", [128, 2, H], BF16, offset=base)
        chb_t = nc.alloc_sbuf_tensor_at("chb", [128, H], I16, offset=base + 32768)
        hqb_t = nc.alloc_sbuf_tensor_at("hqb", [128, H], BF16, offset=base + 40960)
        hb, chb, hqb = hb_t.ap(), chb_t.ap(), hqb_t.ap()
        hqTb = hqTb_t.ap()

        ppq = EE(nc.psum_tensor("ppq", [128, 512], F32))
        ppk = EE(nc.psum_tensor("ppk", [128, 512], F32))
        pkv = EE(nc.psum_tensor("pkv", [128, 2, 1024], F32))
        pf1 = EE(nc.psum_tensor("pf1", [128, 2, 512], F32))

        def SS(name):
            return Sem(EE(nc.semaphore(name)))
        s_x = [SS("s_x0"), SS("s_x1")]
        s_xA = [SS("s_xA0"), SS("s_xA1")]
        s_x2 = [SS("s_x20"), SS("s_x21")]
        s_w = [SS("s_w0"), SS("s_w1")]
        s_wp2 = [SS("s_wp20"), SS("s_wp21")]
        s_wb2a = SS("s_wb2a")
        s_wb2b = SS("s_wb2b")
        s_qk = SS("s_qk")
        s_xnw = [SS("s_xnw0"), SS("s_xnw1")]
        s_xqw = [SS("s_xqw0"), SS("s_xqw1")]
        s_xnT = [SS("s_xnT0"), SS("s_xnT1")]
        s_xqT = [SS("s_xqT0"), SS("s_xqT1")]
        s_qw = [SS("s_qw0"), SS("s_qw1")]
        s_hqw = [SS("s_hqw0"), SS("s_hqw1")]
        s_hqT = [SS("s_hqT0"), SS("s_hqT1")]
        s_qT = [SS("s_qT0"), SS("s_qT1")]
        s_w1q = [SS("s_w1q0"), SS("s_w1q1")]
        s_w2q = [SS("s_w2q0"), SS("s_w2q1")]
        s_wT = SS("s_wT")
        s_w2T = [SS(f"s_w2T{i}") for i in range(4)]
        s_st = [SS("s_st0"), SS("s_st1")]
        s_V = SS("s_V")
        s_S = SS("s_S")
        s_T = SS("s_T")
        s_G = SS("s_G")
        allsems = [s_qk, s_wT, s_V, s_S, s_T, s_G, s_wb2a, s_wb2b] + s_x + s_xA + \
            s_x2 + s_w + s_wp2 + s_xnw + s_xqw + s_xnT + s_xqT + s_qw + s_hqw + \
            s_hqT + s_qT + s_w1q + s_w2q + s_st + s_w2T

        blk = EE(nc.Block())

        col = lambda i: cols[:, i:i + 1]

        def _fidx(s, m):
            return (2 * m + s) if s < 2 else (16 * s + m)

        def _fpar(s, m):
            return _fidx(s, m) % 2

        def run_schedule(sync, vec, sca, ten, gp):
            for s in allsems:
                s.c = 0
            ms = {"wscols": 0}
            state = {"xb_hist": [], "a0_hist": [], "c_prev": None, "f_prev": [],
                     "a1_b": {}}

            def inc(instr, sem, n=1):
                instr.then_inc(sem.h, n)
                sem.c += n

            def mark(tag, sem):
                ms[tag] = sem.c

            def w(eng, sem, tag):
                eng.wait_ge(sem.h, ms[tag])

            def xb_load(eng_sems, kind, s, m, src):
                hist = state["xb_hist"]
                if len(hist) >= 2:
                    pk, ps, pm = hist[-2]
                    if pk == "A1":
                        w(sync, s_V, f"xnq_{ps}_{pm}")
                    else:
                        w(sync, s_V, f"ep1_{ps}_{pm}")
                b = len(hist) % 2
                hist.append((kind, s, m))
                inc(sync.dma_start(out=xb[:, b, :], in_=src), eng_sems[b], 16)
                return b

            def emit_A0(s, m):
                hist0 = state["a0_hist"]
                b = len(hist0) % 2
                if len(hist0) >= 2:
                    ps, pm = hist0[-2]
                    w(sync, s_V, f"lnst_{ps}_{pm}")
                    w(sync, s_S, f"ssq_{ps}_{pm}")
                hist0.append((s, m))
                inc(sync.dma_start(out=xa0[:, b, :], in_=x_d[s, m * 128:(m + 1) * 128, :]),
                    s_x[b], 16)
                mark(f"x0_{s}_{m}", s_x[b])
                ss = s % 2
                w(vec, s_x[b], f"x0_{s}_{m}")
                if m == 0 and s >= 2:
                    w(vec, s_V, f"batch_{s - 2}")
                xv = xa0[:, b, :]
                vec.tensor_reduce(statc[:, ss, 1, m:m + 1], xv, axis=AX.X, op=ALU.max)
                inc(vec.tensor_reduce(statc[:, ss, 2, m:m + 1], xv, axis=AX.X, op=ALU.min), s_V)
                mark(f"lnst_{s}_{m}", s_V)
                w(sca, s_x[b], f"x0_{s}_{m}")
                if m == 0 and s >= 2:
                    w(sca, s_V, f"batch_{s - 2}")
                sca.activation(out=ob[:], in_=xv, func=AF.Identity, bias=ccst[:, 0:1],
                               scale=1.0, accum_out=statc[:, ss, 0, m:m + 1])
                sca.drain()
                sca.activation(out=ob[:], in_=xv, func=AF.Square, bias=ccst[:, 0:1],
                               scale=1.0, accum_out=statc[:, ss, 3, m:m + 1])
                inc(sca.drain(), s_S)
                mark(f"ssq_{s}_{m}", s_S)

            def emit_batch(s):
                ss = s % 2
                sc_ = lambda i: statc[:, ss, i, :]
                dc = lambda i: dcols[:, ss, i, :]
                mneg = cols[:, 0:TC]
                tmp = cols[:, 16:16 + TC]
                w(vec, s_V, f"lnst_{s}_{TC - 1}")
                w(vec, s_S, f"ssq_{s}_{TC - 1}")
                if s >= 2:
                    w(vec, s_S, f"xnu_{s - 2}_{TC - 1}")
                vec.tensor_scalar(out=mneg, in0=sc_(0), scalar1=-1.0 / D, scalar2=None, op0=ALU.mult)
                vec.drain()
                vec.tensor_tensor(out=tmp, in0=mneg, in1=mneg, op=ALU.mult)
                vec.drain()
                vec.scalar_tensor_tensor(out=dc(0), in0=sc_(3), scalar=1.0 / D,
                                         in1=tmp, op0=ALU.mult, op1=ALU.subtract)
                vec.drain()
                inc(vec.tensor_scalar(out=dc(0), in0=dc(0), scalar1=EPS_LN, scalar2=None, op0=ALU.add), s_V)
                mark(f"vpe_{s}", s_V)
                w(sca, s_V, f"vpe_{s}")
                sca.activation(out=tmp, in_=dc(0), func=AF.Ln, bias=ccst[:, 0:1], scale=1.0)
                sca.drain()
                inc(sca.activation(out=dc(0), in_=tmp, func=AF.Exp,
                                   bias=ccst[:, 0:1], scale=-0.5), s_S)
                mark(f"rstd_{s}", s_S)
                w(vec, s_S, f"rstd_{s}")
                vec.tensor_tensor(out=tmp, in0=sc_(1), in1=mneg, op=ALU.add)
                vec.drain()
                vec.scalar_tensor_tensor(out=sc_(2), in0=sc_(2), scalar=-1.0,
                                         in1=mneg, op0=ALU.mult, op1=ALU.subtract)
                vec.drain()
                vec.tensor_tensor(out=tmp, in0=tmp, in1=sc_(2), op=ALU.max)
                vec.drain()
                vec.tensor_tensor(out=sc_(1), in0=tmp, in1=dc(0), op=ALU.mult)
                vec.drain()
                vec.tensor_scalar(out=sc_(1), in0=sc_(1), scalar1=EPS_Q, scalar2=None, op0=ALU.max)
                vec.drain()
                vec.reciprocal(sc_(0), sc_(1))
                vec.drain()
                vec.scalar_tensor_tensor(out=dc(1), in0=sc_(0), scalar=-127.0,
                                         in1=dc(0), op0=ALU.mult, op1=ALU.mult)
                vec.drain()
                vec.tensor_tensor(out=dc(2), in0=mneg, in1=dc(1), op=ALU.mult)
                vec.tensor_tensor(out=dc(3), in0=mneg, in1=dc(0), op=ALU.mult)
                vec.drain()
                vec.tensor_scalar(out=dc(2), in0=dc(2), scalar1=0.5, scalar2=None, op0=ALU.add)
                if s >= 2:
                    vec.tensor_scalar(out=acolT[:, s * TC:(s + 1) * TC], in0=sc_(1),
                                      scalar1=col(C_S1), scalar2=-1.0 / 254.0, op0=ALU.mult, op1=ALU.mult)
                inc(vec.drain(), s_V)
                mark(f"batch_{s}", s_V)

            def emit_A1a(s, m):
                b = xb_load(s_xA, "A1", s, m, x_d[s, m * 128:(m + 1) * 128, :])
                mark(f"xA_{s}_{m}", s_xA[b])
                seq = s * TC + m
                ss = s % 2
                dc = lambda i: dcols[:, ss, i, m:m + 1]
                state["a1_b"][(s, m)] = b
                prev2 = divmod(seq - 2, TC) if seq >= 2 else None
                w(sca, s_xA[b], f"xA_{s}_{m}")
                w(sca, s_V, f"batch_{s}")
                if prev2 is not None:
                    w(sca, s_xnw[b], f"xnw_{prev2[0]}_{prev2[1]}")
                    w(sca, s_T, f"kv_{prev2[0]}_{prev2[1]}")
                xv = xb[:, b, :]
                sca.activation(out=xnb[:, b, :], in_=xv, func=AF.Identity,
                               bias=dc(3), scale=dc(0))
                sca.activation(out=xv, in_=xv, func=AF.Identity,
                               bias=dc(2), scale=dc(1))
                inc(sca.drain(), s_S)
                mark(f"xnu_{s}_{m}", s_S)
                w(vec, s_S, f"xnu_{s}_{m}")
                vec.drain()
                vec.tensor_scalar(out=cb[:], in0=xv, scalar1=C32,
                                  scalar2=C32 + 1.0, op0=ALU.add, op1=ALU.subtract)
                vec.drain()
                vec.tensor_tensor(out=xv, in0=xv, in1=cb[:], op=ALU.subtract)
                vec.drain()
                if prev2 is not None:
                    w(vec, s_xqw[b], f"xqw_{prev2[0]}_{prev2[1]}")
                inc(vec.scalar_tensor_tensor(out=xnqb[:, b, :], in0=xv, scalar=1.0,
                                             in1=cb[:], op0=ALU.is_gt, op1=ALU.add), s_V)
                mark(f"xnq_{s}_{m}", s_V)
                if s >= 2:
                    w(gp, s_xnT[b], f"xnT_{s - 2}_{m}")
                w(gp, s_S, f"xnu_{s}_{m}")
                inc(gp.dma_start(out=xn_scr[m].rearrange("k t c -> t k c"),
                                 in_=xnb[:, b, :].rearrange("t (k c) -> t k c", k=KD)),
                    s_xnw[b], 16)
                mark(f"xnw_{s}_{m}", s_xnw[b])
                if s >= 2:
                    w(gp, s_xqT[_fpar(s - 2, m)], f"xqT_{s - 2}_{m}")
                w(gp, s_V, f"xnq_{s}_{m}")
                inc(gp.dma_start(out=xnq_scr[ss, m].rearrange("k t c -> t k c"),
                                 in_=xnqb[:, b, :].rearrange("t (k c) -> t k c", k=KD)),
                    s_xqw[b], 16)
                mark(f"xqw_{s}_{m}", s_xqw[b])

            def emit_A1c(s, m):
                b = state["a1_b"][(s, m)]
                seq = s * TC + m
                prev = state["c_prev"]
                state["c_prev"] = (s, m)
                prev2 = divmod(seq - 2, TC) if seq >= 2 else None
                w(sync, s_xnw[b], f"xnw_{s}_{m}")
                if prev2 is not None:
                    w(sync, s_T, f"proj_{prev2[0]}_{prev2[1]}")
                inc(sync.dma_start_transpose(out=xnTb[:, b, :],
                                             in_=xn_scr[m].rearrange("k t c -> (k t) c")),
                    s_xnT[b], 16)
                mark(f"xnT_{s}_{m}", s_xnT[b])
                w(ten, s_xnT[b], f"xnT_{s}_{m}")
                if seq == 0:
                    w(ten, s_V, "qk_bf")
                if prev is not None:
                    w(ten, s_S, f"expqk_{prev[0]}_{prev[1]}")
                for kd in range(KD):
                    xt = xnTb[:, b, kd * 128:(kd + 1) * 128]
                    ten.matmul(ppq[:, 0:E], xt, qkb[:, 0, kd, :],
                               start=(kd == 0), stop=(kd == KD - 1))
                    mmk = ten.matmul(ppk[:, 0:E], xt, qkb[:, 1, kd, :],
                                     start=(kd == 0), stop=(kd == KD - 1))
                inc(mmk, s_T)
                mark(f"proj_{s}_{m}", s_T)

            def emit_A1bS(s, m):
                b = state["a1_b"][(s, m)]
                seq = s * TC + m
                prev2 = divmod(seq - 2, TC) if seq >= 2 else None
                w(sca, s_T, f"proj_{s}_{m}")
                if prev2 is not None:
                    w(sca, s_qw[b], f"qw_{prev2[0]}_{prev2[1]}")
                sca.activation(out=qeb[:, b, :], in_=ppq[:, 0:E], func=AF.Exp,
                               bias=ccst[:, 0:1], scale=1.0)
                sca.activation(out=keb[:, b, :], in_=ppk[:, 0:E], func=AF.Exp,
                               bias=ccst[:, 0:1], scale=1.0)
                inc(sca.drain(), s_S)
                mark(f"expqk_{s}_{m}", s_S)

            def emit_A1bT(s, m):
                b = state["a1_b"].pop((s, m))
                w(ten, s_S, f"expqk_{s}_{m}")
                if m == 0 and s >= 1:
                    w(ten, s_V, f"kvev_{s - 1}")
                    w(ten, s_V, "wscols")
                for ec in range(EC):
                    ke = keb[:, b, ec * 128:(ec + 1) * 128]
                    for half in range(2):
                        mmk = ten.matmul(pkv[:, ec, half * 512:(half + 1) * 512], ke,
                                         xnb[:, b, half * 512:(half + 1) * 512],
                                         start=(m == 0), stop=(m == TC - 1))
                inc(mmk, s_T)
                mark(f"kv_{s}_{m}", s_T)
                w(gp, s_S, f"expqk_{s}_{m}")
                inc(gp.dma_start(out=q_scr[s, m].rearrange("k t c -> t k c"),
                                 in_=qeb[:, b, :].rearrange("t (k c) -> t k c", k=EC)),
                    s_qw[b], 16)
                mark(f"qw_{s}_{m}", s_qw[b])

            def emit_kvev(s):
                w(vec, s_T, f"kv_{s}_{TC - 1}")
                for ec in range(EC):
                    vec.tensor_copy(kvb[:, s, ec, 0:1024], pkv[:, ec, :])
                vec.memset(kvb[:, s, 0, 1024:1025], 1.0)
                vec.memset(kvb[:, s, 1, 1024:1025], 1.0)
                inc(vec.drain(), s_V)
                mark(f"kvev_{s}", s_V)

            def tail_vp(ts, tm, tfb):
                c_su = 40 + 3 * tfb
                w(sca, s_V, f"hcols_{ts}_{tm}")
                sca.activation(out=hb[:, tfb, :], in_=hb[:, tfb, :], func=AF.Identity,
                               bias=ccst[:, 1:2], scale=col(c_su))
                inc(sca.drain(), s_S)
                mark(f"vp_{ts}_{tm}", s_S)

            def tail_cd(ts, tm, tfb):
                w(vec, s_S, f"vp_{ts}_{tm}")
                vec.drain()
                inc(vec.tensor_scalar(out=chb[:], in0=hb[:, tfb, :], scalar1=C32,
                                      scalar2=C32 + 1.0, op0=ALU.add, op1=ALU.subtract), s_V)
                mark(f"hc_{ts}_{tm}", s_V)
                vec.drain()
                inc(vec.tensor_tensor(out=dbuf[:, 0:H // 2], in0=hb[:, tfb, 0:H // 2],
                                      in1=chb[:, 0:H // 2], op=ALU.subtract), s_V)
                mark(f"hd_{ts}_{tm}", s_V)
                tprev = state.get("t_prev")
                if tprev is not None:
                    w(gp, s_V, f"hq_{tprev[0]}_{tprev[1]}")
                w(gp, s_V, f"hc_{ts}_{tm}")
                inc(gp.tensor_tensor(out=dbuf[:, H // 2:H], in0=hb[:, tfb, H // 2:H],
                                     in1=chb[:, H // 2:H], op=ALU.subtract), s_G)
                mark(f"hg_{ts}_{tm}", s_G)

            def tail_V2(ts, tm, tfb):
                w(vec, s_G, f"hg_{ts}_{tm}")
                tprev = state.get("t_prev")
                if tprev is not None:
                    w(vec, s_hqw[tprev[2]], f"hqw_{tprev[0]}_{tprev[1]}")
                vec.drain()
                inc(vec.scalar_tensor_tensor(out=hqb[:], in0=dbuf[:], scalar=1.0,
                                             in1=chb[:],
                                             op0=ALU.is_gt, op1=ALU.add), s_V)
                mark(f"hq_{ts}_{tm}", s_V)
                w(gp, s_V, f"hq_{ts}_{tm}")
                hqv = hqb.rearrange("t (k c) -> t k c", k=KH)
                gp.dma_start(out=hq_scr[ts, tm, 0:KH // 2].rearrange("k t c -> t k c"),
                             in_=hqv[:, 0:KH // 2, :]).then_inc(s_hqw[tfb].h, 16)
                s_hqw[tfb].c += 16
                inc(gp.dma_start(out=hq_scr[ts, tm, KH // 2:KH].rearrange("k t c -> t k c"),
                                 in_=hqv[:, KH // 2:KH, :]),
                    s_hqw[tfb], 16)
                mark(f"hqw_{ts}_{tm}", s_hqw[tfb])
                state["t_prev"] = (ts, tm, tfb)

            def flush_tail():
                ft = state.get("f_tail")
                if ft is not None:
                    tail_vp(*ft)
                    tail_cd(*ft)
                    tail_V2(*ft)
                    state["f_tail"] = None

            def emit_FFN1(s, m):
                seq = s * TC + m
                ss = s % 2
                forder = state["f_prev"]
                g = len(forder)
                fb = g % 2
                fprev = forder[-1] if forder else None
                fprev2 = forder[-2] if len(forder) >= 2 else None
                forder.append((s, m))
                w(sync, s_xqw[m % 2], f"xqw_{s}_{m}")
                if fprev2 is not None:
                    w(sync, s_T, f"f1_{fprev2[0]}_{fprev2[1]}_{NH - 1}")
                if g == 0:
                    w(sync, s_wT, "wq1T")
                inc(sync.dma_start_transpose(out=xqTb[:, fb, :],
                                             in_=xnq_scr[ss, m].rearrange("k t c -> (k t) c")),
                    s_xqT[fb], 16)
                mark(f"xqT_{s}_{m}", s_xqT[fb])
                w(ten, s_xqT[fb], f"xqT_{s}_{m}")
                if g == 0:
                    w(ten, s_wT, "wq1T")
                wq1v = wqT.rearrange("p k h -> p (k h)")
                for n in range(NH):
                    if n == 4 and state.get("f_tail") is not None:
                        tail_vp(*state["f_tail"])
                    if n >= 2:
                        w(ten, s_S, f"g1_{s}_{m}_{n - 2}")
                    elif fprev is not None:
                        w(ten, s_S, f"g1_{fprev[0]}_{fprev[1]}_{NH - 2 + n}")
                    for kd in range(KD):
                        mmk = ten.matmul(pf1[:, n % 2, :],
                                         xqTb[:, fb, kd * 128:(kd + 1) * 128],
                                         wq1v[:, kd * H + n * 512: kd * H + (n + 1) * 512],
                                         start=(kd == 0), stop=(kd == KD - 1))
                    inc(mmk, s_T)
                    mark(f"f1_{s}_{m}_{n}", s_T)
                    w(sca, s_T, f"f1_{s}_{m}_{n}")
                    if n == 0 and fprev2 is not None:
                        w(sca, s_V, f"hd_{fprev2[0]}_{fprev2[1]}")
                        w(sca, s_G, f"hg_{fprev2[0]}_{fprev2[1]}")
                    inc(sca.activation(out=hb[:, fb, n * 512:(n + 1) * 512], in_=pf1[:, n % 2, :],
                                       func=AF.Gelu, bias=ccst[:, 0:1],
                                       scale=acolT[:, seq:seq + 1]), s_S)
                    mark(f"g1_{s}_{m}_{n}", s_S)
                ft = state.get("f_tail")
                if ft is not None:
                    tail_cd(*ft)
                w(vec, s_S, f"g1_{s}_{m}_{NH - 1}")
                if fprev2 is not None:
                    w(vec, s_S, f"vp_{fprev2[0]}_{fprev2[1]}")
                c_mx, c_inv, c_su = 38 + 3 * fb, 39 + 3 * fb, 40 + 3 * fb
                vec.reduce_max(col(c_mx), hb[:, fb, :], axis=AX.X, apply_absolute_value=True)
                vec.drain()
                vec.tensor_scalar(out=col(c_mx), in0=col(c_mx), scalar1=EPS_Q, scalar2=None, op0=ALU.max)
                vec.drain()
                vec.reciprocal(col(c_inv), col(c_mx))
                vec.drain()
                vec.tensor_scalar(out=col(c_su), in0=col(c_inv), scalar1=-127.0, scalar2=None, op0=ALU.mult)
                vec.tensor_scalar(out=bcolT[:, seq:seq + 1], in0=col(c_mx),
                                  scalar1=col(C_S2), scalar2=-1.0 / 254.0, op0=ALU.mult, op1=ALU.mult)
                inc(vec.drain(), s_V)
                mark(f"hcols_{s}_{m}", s_V)
                if ft is not None:
                    tail_V2(*ft)
                state["f_tail"] = (s, m, fb)

            def emit_w1p1(c):
                b = c % 2
                if c >= 2:
                    w(sync, s_V, f"wred_0_{c - 2}")
                inc(sync.dma_start(out=wb[:, b, :], in_=w1_d[c * 128:(c + 1) * 128, :]),
                    s_w[b], 16)
                mark(f"wld_0_{c}", s_w[b])
                w(vec, s_w[b], f"wld_0_{c}")
                inc(vec.reduce_sum(wcols[:, c:c + 1], wb[:, b, :], axis=AX.X,
                                   apply_absolute_value=True), s_V)
                mark(f"wred_0_{c}", s_V)

            def emit_w2p1(c):
                if c >= 1:
                    w(sync, s_V, f"wred_1_{c - 1}")
                else:
                    w(sync, s_V, "qk_bf")
                inc(sync.dma_start(out=wb2, in_=w2_d[c * 128:(c + 1) * 128, :]), s_wb2a, 16)
                mark(f"wld_1_{c}", s_wb2a)
                w(vec, s_wb2a, f"wld_1_{c}")
                inc(vec.reduce_sum(wcols[:, 32 + c:33 + c], wb2, axis=AX.X,
                                   apply_absolute_value=True), s_V)
                mark(f"wred_1_{c}", s_V)

            def emit_scalc():
                w(vec, s_V, f"wred_0_{W1C - 1}")
                w(vec, s_V, f"wred_1_{W2C - 1}")
                vec.reduce_sum(col(C_S1), wcols[:, 0:W1C], axis=AX.X)
                vec.reduce_sum(col(C_S2), wcols[:, 32:32 + W2C], axis=AX.X)
                inc(vec.drain(), s_V)
                mark("wcolsum", s_V)
                w(ten, s_V, "wcolsum")
                w(ten, s_V, "kvev_0")
                ten.matmul(pkv[0:1, 1, 0:1], onescol[:], col(C_S1), start=True, stop=True)
                inc(ten.matmul(pkv[0:1, 1, 1:2], onescol[:], col(C_S2), start=True, stop=True), s_T)
                mark("wsred", s_T)
                w(vec, s_T, "wsred")
                vec.tensor_copy(sval[:, 0:2], pkv[0:1, 1, 0:2])
                vec.drain()
                vec.tensor_scalar(out=sval[:, 2:3], in0=sval[:, 0:1], scalar1=0.5 / (H * D),
                                  scalar2=None, op0=ALU.mult)
                vec.tensor_scalar(out=sval[:, 3:4], in0=sval[:, 1:2], scalar1=0.5 / (D * H),
                                  scalar2=None, op0=ALU.mult)
                inc(vec.drain(), s_V)
                mark("wsval", s_V)
                w(ten, s_V, "wsval")
                inc(ten.matmul(pkv[:, 1, 4:6], onesrow[:], sval[:, 2:4], start=True, stop=True), s_T)
                mark("wsbcast", s_T)
                w(vec, s_T, "wsbcast")
                vec.tensor_copy(cols[:, C_TH1P:C_TH1P + 2], pkv[:, 1, 4:6])
                vec.drain()
                vec.tensor_scalar(out=col(C_S1), in0=col(C_TH1P), scalar1=2.0, scalar2=None, op0=ALU.mult)
                vec.tensor_scalar(out=col(C_S2), in0=col(C_TH2P), scalar1=2.0, scalar2=None, op0=ALU.mult)
                vec.tensor_scalar(out=col(C_TH1N), in0=col(C_TH1P), scalar1=-1.0, scalar2=None, op0=ALU.mult)
                vec.tensor_scalar(out=col(C_TH2N), in0=col(C_TH2P), scalar1=-1.0, scalar2=None, op0=ALU.mult)
                vec.drain()
                vec.tensor_scalar(out=acolT[:, 0:TC], in0=statc[:, 0, 1, :],
                                  scalar1=col(C_S1), scalar2=-1.0 / 254.0, op0=ALU.mult, op1=ALU.mult)
                if n_seq > 1:
                    vec.tensor_scalar(out=acolT[:, TC:2 * TC], in0=statc[:, 1, 1, :],
                                      scalar1=col(C_S1), scalar2=-1.0 / 254.0, op0=ALU.mult, op1=ALU.mult)
                inc(vec.drain(), s_V)
                mark("wscols", s_V)

            def emit_wp2(wi, c):
                b = c % 2
                wd = D if wi == 0 else H
                thp = col(C_TH1P) if wi == 0 else col(C_TH2P)
                thn = col(C_TH1N) if wi == 0 else col(C_TH2N)
                s_q = s_w1q if wi == 0 else s_w2q
                scr = wq1_scr if wi == 0 else wq2_scr
                if wi == 1:
                    if c >= 1:
                        if c % 2 == 0:
                            w(sync, s_V, f"wsgV_1_{c - 1}")
                        else:
                            w(sync, s_S, f"wsgn_1_{c - 1}")
                    else:
                        w(sync, s_V, f"wred_1_{W2C - 1}")
                    inc(sync.dma_start(out=wb2, in_=w2_d[c * 128:(c + 1) * 128, :]), s_wb2b, 16)
                else:
                    w(sync, s_S, f"wsgn_0_{c - 2}" if c >= 2 else f"wsgn_1_{W2C - 1}")
                    inc(sync.dma_start(out=wb[:, b, :], in_=w1_d[c * 128:(c + 1) * 128, :]),
                        s_wp2[b], 16)
                mark(f"wld2_{wi}_{c}", s_wb2b if wi == 1 else s_wp2[b])
                vroute = (wi == 1 and c % 2 == 1)
                if not vroute:
                    w(sca, s_wb2b if wi == 1 else s_wp2[b], f"wld2_{wi}_{c}")
                    if c == 0 and wi == 1:
                        w(sca, s_V, "wscols")
                    if c >= 1:
                        w(sca, s_V, f"wtrn_{wi}_{c - 1}")
                    elif wi == 0:
                        w(sca, s_V, f"wtrn_1_{W2C - 1}")
                    src = wb[:, b, :] if wi == 0 else wb2
                    sca.activation(out=wsg[:, 0, 0:wd], in_=src, func=AF.Sign, bias=thn, scale=1.0)
                    sca.activation(out=wsg[:, 1, 0:wd], in_=src, func=AF.Sign, bias=thp, scale=1.0)
                    inc(sca.drain(), s_S)
                    mark(f"wsgn_{wi}_{c}", s_S)
                    w(vec, s_S, f"wsgn_{wi}_{c}")
                else:
                    w(vec, s_wb2b, f"wld2_{wi}_{c}")
                    vec.drain()
                    vec.tensor_scalar(out=wsg[:, 0, 0:wd], in0=wb2, scalar1=thp,
                                      scalar2=2.0, op0=ALU.is_ge, op1=ALU.mult)
                    vec.tensor_scalar(out=wsg[:, 1, 0:wd], in0=wb2, scalar1=thn,
                                      scalar2=-2.0, op0=ALU.is_lt, op1=ALU.mult)
                    inc(vec.drain(), s_V)
                    mark(f"wsgV_{wi}_{c}", s_V)
                    mark(f"wsgn_{wi}_{c}", s_S)
                if c >= 2:
                    w(vec, s_q[b], f"wqw_{wi}_{c - 2}")
                elif wi == 0:
                    w(vec, s_w2q[(W2C - 2 + c) % 2], f"wqw_1_{W2C - 2 + c}")
                inc(vec.tensor_tensor(out=wqo[:, b, 0:wd], in0=wsg[:, 0, 0:wd],
                                      in1=wsg[:, 1, 0:wd], op=ALU.add), s_V)
                mark(f"wtrn_{wi}_{c}", s_V)
                w(gp, s_V, f"wtrn_{wi}_{c}")
                inc(gp.dma_start(out=scr[c * 128:(c + 1) * 128, 0:wd], in_=wqo[:, b, 0:wd]),
                    s_q[b], 16)
                mark(f"wqw_{wi}_{c}", s_q[b])

            def emit_wq1T():
                w(sync, s_w1q[0], f"wqw_0_{W1C - 2}")
                w(sync, s_w1q[1], f"wqw_0_{W1C - 1}")
                w(sync, s_w2q[0], f"wqw_1_{W2C - 2}")
                w(sync, s_w2q[1], f"wqw_1_{W2C - 1}")
                wq1v = wqT.rearrange("p k h -> p (k h)")
                for kd in range(KD):
                    inc(sync.dma_start_transpose(out=wq1v[:, kd * H:(kd + 1) * H],
                                                 in_=wq1_scr[:, kd * 128:(kd + 1) * 128]),
                        s_wT, 16)
                mark("wq1T", s_wT)

            def emit_P2(s, m):
                seq = s * TC + m
                b = m % 2
                prev = divmod(seq - 1, TC) if seq >= 1 else None
                prev2 = divmod(seq - 2, TC) if seq >= 2 else None
                if prev2 is not None:
                    w(sync, s_T, f"f2_{prev2[0]}_{prev2[1]}")
                    w(sync, s_V, f"ep2_{prev2[0]}_{prev2[1]}")
                if seq == 0:
                    w(sync, s_V, f"hq_{n_seq - 1}_{TC - 1}")
                    w(sync, s_G, f"hg_{n_seq - 1}_{TC - 1}")
                    w(sync, s_S, f"vp_{n_seq - 1}_{TC - 1}")
                w(sync, s_hqw[_fpar(s, m)], f"hqw_{s}_{m}")
                inc(sync.dma_start_transpose(out=hqTb[:, b, :],
                                             in_=hq_scr[s, m].rearrange("k t c -> (k t) c")),
                    s_hqT[b], 16)
                mark(f"hqT_{s}_{m}", s_hqT[b])
                w(sync, s_qw[b], f"qw_{s}_{m}")
                inc(sync.dma_start_transpose(out=qTb[:, b, :],
                                             in_=q_scr[s, m].rearrange("k t c -> (k t) c")),
                    s_qT[b], 16)
                mark(f"qT_{s}_{m}", s_qT[b])
                xb_load(s_x2, "P2", s, m, x_d[s, m * 128:(m + 1) * 128, :])
                mark(f"x2_{s}_{m}", s_x2[b])
                pb = seq % 2
                pnum0 = ppq[:] if pb == 0 else pf1[:, 0, :]
                pnum1 = ppk[:] if pb == 0 else pf1[:, 1, :]
                pz = pkv[:, 1, 8:9] if pb == 0 else pkv[:, 1, 520:521]
                w(ten, s_qT[b], f"qT_{s}_{m}")
                if seq == 0:
                    w(ten, s_V, f"kvev_{n_seq - 1}")
                if prev2 is not None:
                    w(ten, s_V, f"ep1_{prev2[0]}_{prev2[1]}")
                if seq == 1:
                    sl, sm = state["f_prev"][-1]
                    w(ten, s_S, f"g1_{sl}_{sm}_{NH - 1}")
                if pb == 1 and prev is not None:
                    w(ten, s_V, f"ep2_{prev[0]}_{prev[1]}")
                for ec in range(EC):
                    qt = qTb[:, b, ec * 128:(ec + 1) * 128]
                    ten.matmul(pnum0, qt, kvb[:, s, ec, 0:512],
                               start=(ec == 0), stop=(ec == EC - 1))
                    ten.matmul(pnum1, qt, kvb[:, s, ec, 512:1024],
                               start=(ec == 0), stop=(ec == EC - 1))
                    mmk = ten.matmul(pz, qt, kvb[:, s, ec, 1024:1025],
                                     start=(ec == 0), stop=(ec == EC - 1))
                inc(mmk, s_T)
                mark(f"num_{s}_{m}", s_T)
                w(ten, s_hqT[b], f"hqT_{s}_{m}")
                if seq == 0:
                    sl, sm = state["f_prev"][-1]
                    w(ten, s_S, f"g1_{sl}_{sm}_{NH - 1}")
                if prev2 is not None:
                    w(ten, s_V, f"ep2_{prev2[0]}_{prev2[1]}")
                if pb == 0 and prev is not None:
                    w(ten, s_V, f"ep1_{prev[0]}_{prev[1]}")
                pf2 = [pf1[:, 0, :], pf1[:, 1, :]] if (pb == 0) else \
                      [pkv[:, 0, 0:512], pkv[:, 0, 512:1024]]
                for kh in range(KH):
                    if seq == 0 and kh % (KH // 4) == 0:
                        g2 = kh // (KH // 4)
                        w(ten, s_w2T[g2], f"wq2Tg_{g2}")
                    ht = hqTb[:, b, kh * 128:(kh + 1) * 128]
                    for half in range(2):
                        mmk = ten.matmul(pf2[half], ht, wqT[:, kh, half * 512:(half + 1) * 512],
                                         start=(kh == 0), stop=(kh == KH - 1))
                inc(mmk, s_T)
                mark(f"f2_{s}_{m}", s_T)
                w(vec, s_T, f"num_{s}_{m}")
                w(vec, s_x2[b], f"x2_{s}_{m}")
                if prev is not None:
                    w(vec, s_st[prev[1] % 2], f"st_{prev[0]}_{prev[1]}")
                zr = zrb[:, pb:pb + 1]
                vec.drain()
                vec.tensor_scalar(out=zr, in0=pz, scalar1=EPS_Z,
                                  scalar2=None, op0=ALU.add)
                vec.drain()
                vec.reciprocal(zr, zr)
                vec.drain()
                vec.scalar_tensor_tensor(out=ob[:, 0:512], in0=pnum0, scalar=zr,
                                         in1=xb[:, b, 0:512], op0=ALU.mult, op1=ALU.add)
                inc(vec.scalar_tensor_tensor(out=ob[:, 512:1024], in0=pnum1, scalar=zr,
                                             in1=xb[:, b, 512:1024], op0=ALU.mult, op1=ALU.add), s_V)
                mark(f"ep1_{s}_{m}", s_V)
                w(vec, s_T, f"f2_{s}_{m}")
                bc = bcolT[:, seq:seq + 1]
                vec.drain()
                vec.scalar_tensor_tensor(out=ob[:, 0:512], in0=pf2[0], scalar=bc,
                                         in1=ob[:, 0:512], op0=ALU.mult, op1=ALU.add)
                inc(vec.scalar_tensor_tensor(out=ob[:, 512:1024], in0=pf2[1], scalar=bc,
                                             in1=ob[:, 512:1024], op0=ALU.mult, op1=ALU.add), s_V)
                mark(f"ep2_{s}_{m}", s_V)
                w(gp, s_V, f"ep2_{s}_{m}")
                inc(gp.dma_start(out=out_d[s, m * 128:(m + 1) * 128, :], in_=ob[:]),
                    s_st[b], 16)
                mark(f"st_{s}_{m}", s_st[b])

            vec.memset(ccst[:, 0:1], 0.0)
            vec.memset(ccst[:, 1:2], 0.5)
            vec.memset(ccst[:, 2:3], 1.0)
            vec.memset(ccst[:, 3:4], -0.5)
            vec.memset(onescol[:], 1.0)
            vec.memset(onesrow[:], 1.0)
            inc(vec.drain(), s_V)
            mark("consts", s_V)
            w(sca, s_V, "consts")
            w(ten, s_V, "consts")
            w(gp, s_V, "consts")
            inc(sync.dma_start(out=wb2.rearrange("t (k e) -> t k e", k=KD)[:, :, 0:E],
                               in_=qp_d.rearrange("(k t) e -> t k e", k=KD)), s_qk, 16)
            inc(sync.dma_start(out=wb2.rearrange("t (k e) -> t k e", k=KD)[:, :, E:2 * E],
                               in_=kp_d.rearrange("(k t) e -> t k e", k=KD)), s_qk, 16)
            mark("qk_ld", s_qk)
            w(vec, s_qk, "qk_ld")
            vec.tensor_copy(qkb[:, 0, :, :], wb2.rearrange("t (k e) -> t k e", k=KD)[:, :, 0:E])
            vec.tensor_copy(qkb[:, 1, :, :], wb2.rearrange("t (k e) -> t k e", k=KD)[:, :, E:2 * E])
            inc(vec.drain(), s_V)
            mark("qk_bf", s_V)

            def runjob(j):
                kind = j[0]
                if kind == "w1p1":
                    emit_w1p1(j[1])
                elif kind == "w2p1":
                    emit_w2p1(j[1])
                elif kind == "wp2":
                    emit_wp2(j[1], j[2])
                elif kind == "A0":
                    emit_A0(j[1], j[2])
                elif kind == "A1a":
                    emit_A1a(j[1], j[2])
                elif kind == "A1c":
                    emit_A1c(j[1], j[2])
                elif kind == "A1b":
                    emit_A1bS(j[1], j[2])
                    emit_A1bT(j[1], j[2])
                elif kind == "batch":
                    emit_batch(j[1])

            wjobs = [("w1p1", c) for c in range(W1C)] + [("w2p1", c) for c in range(W2C)]
            ajobs = [("A0", 0, m) for m in range(TC)] + [("batch", 0)]
            for m in range(TC):
                if m >= 2:
                    ajobs.append(("A1b", 0, m - 2))
                ajobs.append(("A1a", 0, m))
                if m >= 1:
                    ajobs.append(("A1c", 0, m - 1))
                ajobs.append(("A0", 1, m))
            ajobs += [("A1b", 0, TC - 2), ("A1c", 0, TC - 1),
                      ("A1b", 0, TC - 1), ("batch", 1)]
            wi_ = ai_ = 0
            while wi_ < len(wjobs) or ai_ < len(ajobs):
                if ai_ < len(ajobs):
                    runjob(ajobs[ai_]); ai_ += 1
                if wi_ < len(wjobs):
                    runjob(wjobs[wi_]); wi_ += 1
            emit_kvev(0)
            emit_scalc()
            wjobs = [("wp2", 1, c) for c in range(W2C)] + [("wp2", 0, c) for c in range(W1C)]
            ajobs = []
            for m in range(TC):
                if m >= 2:
                    ajobs.append(("A1b", 1, m - 2))
                ajobs.append(("A1a", 1, m))
                if m >= 1:
                    ajobs.append(("A1c", 1, m - 1))
                ajobs += [("A0", 2, m), ("A0", 3, m)]
            ajobs += [("A1b", 1, TC - 2), ("A1c", 1, TC - 1),
                      ("A1b", 1, TC - 1), ("batch", 2), ("batch", 3)]
            wi_ = ai_ = 0
            while wi_ < len(wjobs) or ai_ < len(ajobs):
                if ai_ < len(ajobs):
                    runjob(ajobs[ai_]); ai_ += 1
                if wi_ < len(wjobs):
                    runjob(wjobs[wi_]); wi_ += 1
            emit_kvev(1)
            emit_wq1T()
            for m in range(TC):
                emit_FFN1(0, m)
                emit_FFN1(1, m)
            NK = 2 * TC
            a1map = lambda k: (2 + k // TC, k % TC)
            emit_A1a(*a1map(0))
            if NK > 1:
                emit_A1a(*a1map(1))
            emit_A1c(*a1map(0))
            emit_A1bS(*a1map(0))
            emit_A1bT(*a1map(0))
            if NK > 2:
                emit_A1a(*a1map(2))
            if NK > 1:
                emit_A1c(*a1map(1))
            for k in range(NK):
                if k + 1 < NK:
                    emit_A1bS(*a1map(k + 1))
                emit_FFN1(*a1map(k))
                if k + 1 < NK:
                    emit_A1bT(*a1map(k + 1))
                    if (k + 1) % TC == TC - 1:
                        emit_kvev(2 + (k + 1) // TC)
                if k + 2 < NK:
                    emit_A1c(*a1map(k + 2))
                if k + 3 < NK:
                    emit_A1a(*a1map(k + 3))
            flush_tail()
            w(sync, s_T, f"f1_3_{TC - 1}_{NH - 1}")
            for kh in range(KH):
                g2 = kh // (KH // 4)
                inc(sync.dma_start_transpose(out=wqT[:, kh, :],
                                             in_=wq2_scr[:, kh * 128:(kh + 1) * 128]),
                    s_w2T[g2], 16)
                if kh % (KH // 4) == KH // 4 - 1:
                    mark(f"wq2Tg_{g2}", s_w2T[g2])
            for s in range(n_seq):
                for m in range(TC):
                    emit_P2(s, m)
            w(sync, s_st[0], f"st_{n_seq - 1}_{TC - 2}")
            w(sync, s_st[1], f"st_{n_seq - 1}_{TC - 1}")

        fake = _Proxy(None)

        @blk.sync
        def _(sync):
            run_schedule(_Proxy(sync), fake, fake, fake, fake)

        @blk.vector
        def _(vector):
            run_schedule(fake, _Proxy(vector), fake, fake, fake)

        @blk.scalar
        def _(scalar):
            run_schedule(fake, fake, _Proxy(scalar), fake, fake)

        @blk.tensor
        def _(tensor):
            run_schedule(fake, fake, fake, _Proxy(tensor), fake)

        @blk.gpsimd
        def _(gpsimd):
            run_schedule(fake, fake, fake, fake, _Proxy(gpsimd))

    return nc


_NC_CACHE = {}


def _get_nc(n_seq, T):
    key = (n_seq, T)
    if key not in _NC_CACHE:
        _NC_CACHE[key] = build_nc(n_seq, T)
    return _NC_CACHE[key]


def kernel(L0_attn_x, L0_attn_q_proj, L0_attn_k_proj, L0_ffn_w1, L0_ffn_w2):
    x = np.ascontiguousarray(np.asarray(L0_attn_x, dtype=np.float32))
    qp = np.ascontiguousarray(np.asarray(L0_attn_q_proj, dtype=np.float32))
    kp = np.ascontiguousarray(np.asarray(L0_attn_k_proj, dtype=np.float32))
    w1 = np.ascontiguousarray(np.asarray(L0_ffn_w1, dtype=np.float32))
    w2 = np.ascontiguousarray(np.asarray(L0_ffn_w2, dtype=np.float32))
    B, T, d = x.shape
    n_seq = B // N_CORES
    nc = _get_nc(n_seq, T)
    in_maps = []
    for c in range(N_CORES):
        in_maps.append({
            "x": x[c * n_seq:(c + 1) * n_seq],
            "q_proj": qp, "k_proj": kp, "w1": w1, "w2": w2,
        })
    res = run_bass_kernel_spmd(nc, in_maps, core_ids=list(range(N_CORES)))
    return np.concatenate([np.asarray(r["out"]) for r in res.results], axis=0)


# version 30
# speedup vs baseline: 1.0122x; 1.0080x over previous
import numpy as np
import concourse.bass as bass
import concourse.mybir as mybir
from concourse.alu_op_type import AluOpType as ALU
from concourse.bass_utils import run_bass_kernel_spmd
from contextlib import ExitStack

F32 = mybir.dt.float32
BF16 = mybir.dt.bfloat16
I16 = mybir.dt.int16
AF = mybir.ActivationFunctionType
AX = mybir.AxisListType

C32 = 12582912.0
D, E, H = 1024, 256, 4096
KD, EC, KH, NH = 8, 2, 32, 8
W1C, W2C = 32, 8
EPS_LN = 1e-5
EPS_Q = 1e-5
EPS_Z = 1e-8

N_CORES = 8

C_S1, C_S2, C_TH1P, C_TH2P, C_TH1N, C_TH2N = 32, 33, 34, 35, 36, 37


class _FakeInstr:
    def then_inc(self, *a, **k):
        return self


_FAKE = _FakeInstr()


class _Proxy:
    def __init__(self, eng):
        self._e = eng

    def __getattr__(self, name):
        if self._e is None:
            return lambda *a, **k: _FAKE
        return getattr(self._e, name)


class Sem:
    def __init__(self, handle):
        self.h = handle
        self.c = 0


def build_nc(n_seq, T):
    TC = T // 128
    assert TC % 2 == 0
    NCH = n_seq * TC
    nc = bass.Bass("TRN2", target_bir_lowering=False)

    x_d = nc.dram_tensor("x", [n_seq, T, D], F32, kind="ExternalInput").ap()
    qp_d = nc.dram_tensor("q_proj", [D, E], F32, kind="ExternalInput").ap()
    kp_d = nc.dram_tensor("k_proj", [D, E], F32, kind="ExternalInput").ap()
    w1_d = nc.dram_tensor("w1", [H, D], F32, kind="ExternalInput").ap()
    w2_d = nc.dram_tensor("w2", [D, H], F32, kind="ExternalInput").ap()
    out_d = nc.dram_tensor("out", [n_seq, T, D], F32, kind="ExternalOutput").ap()

    xn_scr = nc.dram_tensor("xn_scr", [TC, KD, 128, 128], BF16).ap()
    xnq_scr = nc.dram_tensor("xnq_scr", [2, TC, KD, 128, 128], BF16).ap()
    q_scr = nc.dram_tensor("q_scr", [n_seq, TC, EC, 128, 128], BF16).ap()
    hq_scr = nc.dram_tensor("hq_scr", [n_seq, TC, KH, 128, 128], BF16).ap()
    wq1_scr = nc.dram_tensor("wq1_scr", [H, D], BF16).ap()
    wq2_scr = nc.dram_tensor("wq2_scr", [D, H], BF16).ap()

    st = ExitStack()
    EE = st.enter_context
    with st:
        wqT = EE(nc.sbuf_tensor("wqT", [128, KH, 1024], BF16))
        qkb = EE(nc.sbuf_tensor("qkb", [128, 2, KD, E], BF16))
        kvb = EE(nc.sbuf_tensor("kvb", [128, n_seq, EC, 1025], BF16))
        xb = EE(nc.sbuf_tensor("xb", [128, 2, D], F32))
        xa0 = EE(nc.sbuf_tensor("xa0", [128, 2, D], F32))
        xnb = EE(nc.sbuf_tensor("xnb", [128, 2, D], BF16))
        xnTb = EE(nc.sbuf_tensor("xnTb", [128, 2, D], BF16))
        xqTb = EE(nc.sbuf_tensor("xqTb", [128, 2, D], BF16))
        cb = EE(nc.sbuf_tensor("cb", [128, D], I16))
        dbuf = EE(nc.sbuf_tensor("dbuf", [128, H], BF16))
        xnqb = EE(nc.sbuf_tensor("xnqb", [128, 2, D], BF16))
        qeb = EE(nc.sbuf_tensor("qeb", [128, 2, E], BF16))
        keb = EE(nc.sbuf_tensor("keb", [128, 2, E], BF16))
        qTb = EE(nc.sbuf_tensor("qTb", [128, 2, E], BF16))
        ob = EE(nc.sbuf_tensor("ob", [128, D], F32))
        cols = EE(nc.sbuf_tensor("cols", [128, 64], F32))
        statc = EE(nc.sbuf_tensor("statc", [128, 2, 4, TC], F32))
        dcols = EE(nc.sbuf_tensor("dcols", [128, 2, 4, TC], F32))
        ccst = EE(nc.sbuf_tensor("ccst", [128, 4], F32))
        acolT = EE(nc.sbuf_tensor("acolT", [128, NCH], F32))
        bcolT = EE(nc.sbuf_tensor("bcolT", [128, NCH], F32))
        zrb = EE(nc.sbuf_tensor("zrb", [128, 2], F32))
        wcols = EE(nc.sbuf_tensor("wcols", [128, 40], F32))
        onescol = EE(nc.sbuf_tensor("onescol", [128, 1], F32))
        onesrow = EE(nc.sbuf_tensor("onesrow", [1, 128], F32))
        sval = EE(nc.sbuf_tensor("sval", [1, 4], F32))

        ARENA = 57344
        ph = EE(nc.sbuf_tensor("arena", [128, ARENA], mybir.dt.uint8))
        base = nc.lookup_mloc(ph).addr
        wb_t = nc.alloc_sbuf_tensor_at("wb", [128, 2, D], F32, offset=base)
        wb2_t = nc.alloc_sbuf_tensor_at("wb2", [128, H], F32, offset=base + 8192)
        wsg_t = nc.alloc_sbuf_tensor_at("wsg", [128, 2, H], BF16, offset=base + 24576)
        wqo_t = nc.alloc_sbuf_tensor_at("wqo", [128, 2, H], BF16, offset=base + 40960)
        wb, wb2, wsg, wqo = wb_t.ap(), wb2_t.ap(), wsg_t.ap(), wqo_t.ap()
        hb_t = nc.alloc_sbuf_tensor_at("hb", [128, 2, H], F32, offset=base)
        hqTb_t = nc.alloc_sbuf_tensor_at("hqTb", [128, 2, H], BF16, offset=base)
        chb_t = nc.alloc_sbuf_tensor_at("chb", [128, H], I16, offset=base + 32768)
        hqb_t = nc.alloc_sbuf_tensor_at("hqb", [128, H], BF16, offset=base + 40960)
        hb, chb, hqb = hb_t.ap(), chb_t.ap(), hqb_t.ap()
        hqTb = hqTb_t.ap()

        ppq = EE(nc.psum_tensor("ppq", [128, 512], F32))
        ppk = EE(nc.psum_tensor("ppk", [128, 512], F32))
        pkv = EE(nc.psum_tensor("pkv", [128, 2, 1024], F32))
        pf1 = EE(nc.psum_tensor("pf1", [128, 2, 512], F32))

        def SS(name):
            return Sem(EE(nc.semaphore(name)))
        s_x = [SS("s_x0"), SS("s_x1")]
        s_xA = [SS("s_xA0"), SS("s_xA1")]
        s_x2 = [SS("s_x20"), SS("s_x21")]
        s_w = [SS("s_w0"), SS("s_w1")]
        s_wp2 = [SS("s_wp20"), SS("s_wp21")]
        s_wb2a = SS("s_wb2a")
        s_wb2b = SS("s_wb2b")
        s_qk = SS("s_qk")
        s_xnw = [SS("s_xnw0"), SS("s_xnw1")]
        s_xqw = [SS("s_xqw0"), SS("s_xqw1")]
        s_xnT = [SS("s_xnT0"), SS("s_xnT1")]
        s_xqT = [SS("s_xqT0"), SS("s_xqT1")]
        s_qw = [SS("s_qw0"), SS("s_qw1")]
        s_hqw = [SS("s_hqw0"), SS("s_hqw1")]
        s_hqT = [SS("s_hqT0"), SS("s_hqT1")]
        s_qT = [SS("s_qT0"), SS("s_qT1")]
        s_w1q = [SS("s_w1q0"), SS("s_w1q1")]
        s_w2q = [SS("s_w2q0"), SS("s_w2q1")]
        s_wT = SS("s_wT")
        s_w2T = [SS(f"s_w2T{i}") for i in range(4)]
        s_st = [SS("s_st0"), SS("s_st1")]
        s_V = SS("s_V")
        s_S = SS("s_S")
        s_T = SS("s_T")
        s_G = SS("s_G")
        allsems = [s_qk, s_wT, s_V, s_S, s_T, s_G, s_wb2a, s_wb2b] + s_x + s_xA + \
            s_x2 + s_w + s_wp2 + s_xnw + s_xqw + s_xnT + s_xqT + s_qw + s_hqw + \
            s_hqT + s_qT + s_w1q + s_w2q + s_st + s_w2T

        blk = EE(nc.Block())

        col = lambda i: cols[:, i:i + 1]

        def _fidx(s, m):
            return (2 * m + s) if s < 2 else (16 * s + m)

        def _fpar(s, m):
            return _fidx(s, m) % 2

        def run_schedule(sync, vec, sca, ten, gp):
            for s in allsems:
                s.c = 0
            ms = {"wscols": 0}
            state = {"xb_hist": [], "a0_hist": [], "c_prev": None, "f_prev": [],
                     "a1_b": {}}

            def inc(instr, sem, n=1):
                instr.then_inc(sem.h, n)
                sem.c += n

            def mark(tag, sem):
                ms[tag] = sem.c

            def w(eng, sem, tag):
                eng.wait_ge(sem.h, ms[tag])

            def xb_load(eng_sems, kind, s, m, src):
                hist = state["xb_hist"]
                if len(hist) >= 2:
                    pk, ps, pm = hist[-2]
                    if pk == "A1":
                        w(sync, s_V, f"xnq_{ps}_{pm}")
                    else:
                        w(sync, s_V, f"ep1_{ps}_{pm}")
                b = len(hist) % 2
                hist.append((kind, s, m))
                inc(sync.dma_start(out=xb[:, b, :], in_=src), eng_sems[b], 16)
                return b

            def emit_A0(s, m):
                hist0 = state["a0_hist"]
                b = len(hist0) % 2
                if len(hist0) >= 2:
                    ps, pm = hist0[-2]
                    w(sync, s_V, f"lnst_{ps}_{pm}")
                    w(sync, s_S, f"ssq_{ps}_{pm}")
                hist0.append((s, m))
                inc(sync.dma_start(out=xa0[:, b, :], in_=x_d[s, m * 128:(m + 1) * 128, :]),
                    s_x[b], 16)
                mark(f"x0_{s}_{m}", s_x[b])
                ss = s % 2
                w(vec, s_x[b], f"x0_{s}_{m}")
                if m == 0 and s >= 2:
                    w(vec, s_V, f"batch_{s - 2}")
                xv = xa0[:, b, :]
                vec.tensor_reduce(statc[:, ss, 1, m:m + 1], xv, axis=AX.X, op=ALU.max)
                inc(vec.tensor_reduce(statc[:, ss, 2, m:m + 1], xv, axis=AX.X, op=ALU.min), s_V)
                mark(f"lnst_{s}_{m}", s_V)
                w(sca, s_x[b], f"x0_{s}_{m}")
                if m == 0 and s >= 2:
                    w(sca, s_V, f"batch_{s - 2}")
                sca.activation(out=ob[:], in_=xv, func=AF.Identity, bias=ccst[:, 0:1],
                               scale=1.0, accum_out=statc[:, ss, 0, m:m + 1])
                sca.drain()
                sca.activation(out=ob[:], in_=xv, func=AF.Square, bias=ccst[:, 0:1],
                               scale=1.0, accum_out=statc[:, ss, 3, m:m + 1])
                inc(sca.drain(), s_S)
                mark(f"ssq_{s}_{m}", s_S)

            def emit_batch(s):
                ss = s % 2
                sc_ = lambda i: statc[:, ss, i, :]
                dc = lambda i: dcols[:, ss, i, :]
                mneg = cols[:, 0:TC]
                tmp = cols[:, 16:16 + TC]
                w(vec, s_V, f"lnst_{s}_{TC - 1}")
                w(vec, s_S, f"ssq_{s}_{TC - 1}")
                if s >= 2:
                    w(vec, s_S, f"xnu_{s - 2}_{TC - 1}")
                vec.tensor_scalar(out=mneg, in0=sc_(0), scalar1=-1.0 / D, scalar2=None, op0=ALU.mult)
                vec.drain()
                vec.tensor_tensor(out=tmp, in0=mneg, in1=mneg, op=ALU.mult)
                vec.drain()
                vec.scalar_tensor_tensor(out=dc(0), in0=sc_(3), scalar=1.0 / D,
                                         in1=tmp, op0=ALU.mult, op1=ALU.subtract)
                vec.drain()
                inc(vec.tensor_scalar(out=dc(0), in0=dc(0), scalar1=EPS_LN, scalar2=None, op0=ALU.add), s_V)
                mark(f"vpe_{s}", s_V)
                w(sca, s_V, f"vpe_{s}")
                sca.activation(out=tmp, in_=dc(0), func=AF.Ln, bias=ccst[:, 0:1], scale=1.0)
                sca.drain()
                inc(sca.activation(out=dc(0), in_=tmp, func=AF.Exp,
                                   bias=ccst[:, 0:1], scale=-0.5), s_S)
                mark(f"rstd_{s}", s_S)
                w(vec, s_S, f"rstd_{s}")
                vec.tensor_tensor(out=tmp, in0=sc_(1), in1=mneg, op=ALU.add)
                vec.drain()
                vec.scalar_tensor_tensor(out=sc_(2), in0=sc_(2), scalar=-1.0,
                                         in1=mneg, op0=ALU.mult, op1=ALU.subtract)
                vec.drain()
                vec.tensor_tensor(out=tmp, in0=tmp, in1=sc_(2), op=ALU.max)
                vec.drain()
                vec.tensor_tensor(out=sc_(1), in0=tmp, in1=dc(0), op=ALU.mult)
                vec.drain()
                vec.tensor_scalar(out=sc_(1), in0=sc_(1), scalar1=EPS_Q, scalar2=None, op0=ALU.max)
                vec.drain()
                vec.reciprocal(sc_(0), sc_(1))
                vec.drain()
                vec.scalar_tensor_tensor(out=dc(1), in0=sc_(0), scalar=-127.0,
                                         in1=dc(0), op0=ALU.mult, op1=ALU.mult)
                vec.drain()
                vec.tensor_tensor(out=dc(2), in0=mneg, in1=dc(1), op=ALU.mult)
                vec.tensor_tensor(out=dc(3), in0=mneg, in1=dc(0), op=ALU.mult)
                vec.drain()
                vec.tensor_scalar(out=dc(2), in0=dc(2), scalar1=0.5, scalar2=None, op0=ALU.add)
                if s >= 2:
                    vec.tensor_scalar(out=acolT[:, s * TC:(s + 1) * TC], in0=sc_(1),
                                      scalar1=col(C_S1), scalar2=-1.0 / 254.0, op0=ALU.mult, op1=ALU.mult)
                inc(vec.drain(), s_V)
                mark(f"batch_{s}", s_V)

            def emit_A1a(s, m):
                b = xb_load(s_xA, "A1", s, m, x_d[s, m * 128:(m + 1) * 128, :])
                mark(f"xA_{s}_{m}", s_xA[b])
                seq = s * TC + m
                ss = s % 2
                dc = lambda i: dcols[:, ss, i, m:m + 1]
                state["a1_b"][(s, m)] = b
                prev2 = divmod(seq - 2, TC) if seq >= 2 else None
                xv = xb[:, b, :]
                s_xnu = s_S if s < 2 else s_V
                if s < 2:
                    w(sca, s_xA[b], f"xA_{s}_{m}")
                    w(sca, s_V, f"batch_{s}")
                    if prev2 is not None:
                        w(sca, s_xnw[b], f"xnw_{prev2[0]}_{prev2[1]}")
                        w(sca, s_T, f"kv_{prev2[0]}_{prev2[1]}")
                    sca.activation(out=xnb[:, b, :], in_=xv, func=AF.Identity,
                                   bias=dc(3), scale=dc(0))
                    sca.activation(out=xv, in_=xv, func=AF.Identity,
                                   bias=dc(2), scale=dc(1))
                    inc(sca.drain(), s_S)
                    mark(f"xnu_{s}_{m}", s_S)
                    w(vec, s_S, f"xnu_{s}_{m}")
                    vec.drain()
                else:
                    w(vec, s_xA[b], f"xA_{s}_{m}")
                    if prev2 is not None:
                        w(vec, s_xnw[b], f"xnw_{prev2[0]}_{prev2[1]}")
                        w(vec, s_T, f"kv_{prev2[0]}_{prev2[1]}")
                    vec.drain()
                    vec.tensor_scalar(out=xnb[:, b, :], in0=xv, scalar1=dc(0),
                                      scalar2=dc(3), op0=ALU.mult, op1=ALU.add)
                    vec.drain()
                    inc(vec.tensor_scalar(out=xv, in0=xv, scalar1=dc(1),
                                          scalar2=dc(2), op0=ALU.mult, op1=ALU.add), s_V)
                    mark(f"xnu_{s}_{m}", s_V)
                    vec.drain()
                vec.tensor_scalar(out=cb[:], in0=xv, scalar1=C32,
                                  scalar2=C32 + 1.0, op0=ALU.add, op1=ALU.subtract)
                vec.drain()
                vec.tensor_tensor(out=xv, in0=xv, in1=cb[:], op=ALU.subtract)
                vec.drain()
                if prev2 is not None:
                    w(vec, s_xqw[b], f"xqw_{prev2[0]}_{prev2[1]}")
                inc(vec.scalar_tensor_tensor(out=xnqb[:, b, :], in0=xv, scalar=1.0,
                                             in1=cb[:], op0=ALU.is_gt, op1=ALU.add), s_V)
                mark(f"xnq_{s}_{m}", s_V)
                if s >= 2:
                    w(gp, s_xnT[b], f"xnT_{s - 2}_{m}")
                w(gp, s_xnu, f"xnu_{s}_{m}")
                inc(gp.dma_start(out=xn_scr[m].rearrange("k t c -> t k c"),
                                 in_=xnb[:, b, :].rearrange("t (k c) -> t k c", k=KD)),
                    s_xnw[b], 16)
                mark(f"xnw_{s}_{m}", s_xnw[b])
                if s >= 2:
                    w(gp, s_xqT[_fpar(s - 2, m)], f"xqT_{s - 2}_{m}")
                w(gp, s_V, f"xnq_{s}_{m}")
                inc(gp.dma_start(out=xnq_scr[ss, m].rearrange("k t c -> t k c"),
                                 in_=xnqb[:, b, :].rearrange("t (k c) -> t k c", k=KD)),
                    s_xqw[b], 16)
                mark(f"xqw_{s}_{m}", s_xqw[b])

            def emit_A1c(s, m):
                b = state["a1_b"][(s, m)]
                seq = s * TC + m
                prev = state["c_prev"]
                state["c_prev"] = (s, m)
                prev2 = divmod(seq - 2, TC) if seq >= 2 else None
                w(sync, s_xnw[b], f"xnw_{s}_{m}")
                if prev2 is not None:
                    w(sync, s_T, f"proj_{prev2[0]}_{prev2[1]}")
                inc(sync.dma_start_transpose(out=xnTb[:, b, :],
                                             in_=xn_scr[m].rearrange("k t c -> (k t) c")),
                    s_xnT[b], 16)
                mark(f"xnT_{s}_{m}", s_xnT[b])
                w(ten, s_xnT[b], f"xnT_{s}_{m}")
                if seq == 0:
                    w(ten, s_V, "qk_bf")
                if prev is not None:
                    w(ten, s_S, f"expqk_{prev[0]}_{prev[1]}")
                for kd in range(KD):
                    xt = xnTb[:, b, kd * 128:(kd + 1) * 128]
                    ten.matmul(ppq[:, 0:E], xt, qkb[:, 0, kd, :],
                               start=(kd == 0), stop=(kd == KD - 1))
                    mmk = ten.matmul(ppk[:, 0:E], xt, qkb[:, 1, kd, :],
                                     start=(kd == 0), stop=(kd == KD - 1))
                inc(mmk, s_T)
                mark(f"proj_{s}_{m}", s_T)

            def emit_A1bS(s, m):
                b = state["a1_b"][(s, m)]
                seq = s * TC + m
                prev2 = divmod(seq - 2, TC) if seq >= 2 else None
                w(sca, s_T, f"proj_{s}_{m}")
                if prev2 is not None:
                    w(sca, s_qw[b], f"qw_{prev2[0]}_{prev2[1]}")
                sca.activation(out=qeb[:, b, :], in_=ppq[:, 0:E], func=AF.Exp,
                               bias=ccst[:, 0:1], scale=1.0)
                sca.activation(out=keb[:, b, :], in_=ppk[:, 0:E], func=AF.Exp,
                               bias=ccst[:, 0:1], scale=1.0)
                inc(sca.drain(), s_S)
                mark(f"expqk_{s}_{m}", s_S)

            def emit_A1bT(s, m):
                b = state["a1_b"].pop((s, m))
                w(ten, s_S, f"expqk_{s}_{m}")
                if m == 0 and s >= 1:
                    w(ten, s_V, f"kvev_{s - 1}")
                    w(ten, s_V, "wscols")
                for ec in range(EC):
                    ke = keb[:, b, ec * 128:(ec + 1) * 128]
                    for half in range(2):
                        mmk = ten.matmul(pkv[:, ec, half * 512:(half + 1) * 512], ke,
                                         xnb[:, b, half * 512:(half + 1) * 512],
                                         start=(m == 0), stop=(m == TC - 1))
                inc(mmk, s_T)
                mark(f"kv_{s}_{m}", s_T)
                w(gp, s_S, f"expqk_{s}_{m}")
                inc(gp.dma_start(out=q_scr[s, m].rearrange("k t c -> t k c"),
                                 in_=qeb[:, b, :].rearrange("t (k c) -> t k c", k=EC)),
                    s_qw[b], 16)
                mark(f"qw_{s}_{m}", s_qw[b])

            def emit_kvev(s):
                w(vec, s_T, f"kv_{s}_{TC - 1}")
                for ec in range(EC):
                    vec.tensor_copy(kvb[:, s, ec, 0:1024], pkv[:, ec, :])
                vec.memset(kvb[:, s, 0, 1024:1025], 1.0)
                vec.memset(kvb[:, s, 1, 1024:1025], 1.0)
                inc(vec.drain(), s_V)
                mark(f"kvev_{s}", s_V)

            def tail_vp(ts, tm, tfb):
                c_su = 40 + 3 * tfb
                w(sca, s_V, f"hcols_{ts}_{tm}")
                sca.activation(out=hb[:, tfb, :], in_=hb[:, tfb, :], func=AF.Identity,
                               bias=ccst[:, 1:2], scale=col(c_su))
                inc(sca.drain(), s_S)
                mark(f"vp_{ts}_{tm}", s_S)

            def tail_cd(ts, tm, tfb):
                w(vec, s_S, f"vp_{ts}_{tm}")
                vec.drain()
                inc(vec.tensor_scalar(out=chb[:], in0=hb[:, tfb, :], scalar1=C32,
                                      scalar2=C32 + 1.0, op0=ALU.add, op1=ALU.subtract), s_V)
                mark(f"hc_{ts}_{tm}", s_V)
                vec.drain()
                inc(vec.tensor_tensor(out=dbuf[:, 0:H // 2], in0=hb[:, tfb, 0:H // 2],
                                      in1=chb[:, 0:H // 2], op=ALU.subtract), s_V)
                mark(f"hd_{ts}_{tm}", s_V)
                tprev = state.get("t_prev")
                if tprev is not None:
                    w(gp, s_V, f"hq_{tprev[0]}_{tprev[1]}")
                w(gp, s_V, f"hc_{ts}_{tm}")
                inc(gp.tensor_tensor(out=dbuf[:, H // 2:H], in0=hb[:, tfb, H // 2:H],
                                     in1=chb[:, H // 2:H], op=ALU.subtract), s_G)
                mark(f"hg_{ts}_{tm}", s_G)

            def tail_V2(ts, tm, tfb):
                w(vec, s_G, f"hg_{ts}_{tm}")
                tprev = state.get("t_prev")
                if tprev is not None:
                    w(vec, s_hqw[tprev[2]], f"hqw_{tprev[0]}_{tprev[1]}")
                vec.drain()
                inc(vec.scalar_tensor_tensor(out=hqb[:], in0=dbuf[:], scalar=1.0,
                                             in1=chb[:],
                                             op0=ALU.is_gt, op1=ALU.add), s_V)
                mark(f"hq_{ts}_{tm}", s_V)
                w(gp, s_V, f"hq_{ts}_{tm}")
                hqv = hqb.rearrange("t (k c) -> t k c", k=KH)
                gp.dma_start(out=hq_scr[ts, tm, 0:KH // 2].rearrange("k t c -> t k c"),
                             in_=hqv[:, 0:KH // 2, :]).then_inc(s_hqw[tfb].h, 16)
                s_hqw[tfb].c += 16
                inc(gp.dma_start(out=hq_scr[ts, tm, KH // 2:KH].rearrange("k t c -> t k c"),
                                 in_=hqv[:, KH // 2:KH, :]),
                    s_hqw[tfb], 16)
                mark(f"hqw_{ts}_{tm}", s_hqw[tfb])
                state["t_prev"] = (ts, tm, tfb)

            def flush_tail():
                ft = state.get("f_tail")
                if ft is not None:
                    tail_vp(*ft)
                    tail_cd(*ft)
                    tail_V2(*ft)
                    state["f_tail"] = None

            def emit_FFN1(s, m):
                seq = s * TC + m
                ss = s % 2
                forder = state["f_prev"]
                g = len(forder)
                fb = g % 2
                fprev = forder[-1] if forder else None
                fprev2 = forder[-2] if len(forder) >= 2 else None
                forder.append((s, m))
                w(sync, s_xqw[m % 2], f"xqw_{s}_{m}")
                if fprev2 is not None:
                    w(sync, s_T, f"f1_{fprev2[0]}_{fprev2[1]}_{NH - 1}")
                if g == 0:
                    w(sync, s_wT, "wq1T")
                inc(sync.dma_start_transpose(out=xqTb[:, fb, :],
                                             in_=xnq_scr[ss, m].rearrange("k t c -> (k t) c")),
                    s_xqT[fb], 16)
                mark(f"xqT_{s}_{m}", s_xqT[fb])
                w(ten, s_xqT[fb], f"xqT_{s}_{m}")
                if g == 0:
                    w(ten, s_wT, "wq1T")
                wq1v = wqT.rearrange("p k h -> p (k h)")
                for n in range(NH):
                    if n == 4 and state.get("f_tail") is not None:
                        tail_vp(*state["f_tail"])
                    if n >= 2:
                        w(ten, s_S, f"g1_{s}_{m}_{n - 2}")
                    elif fprev is not None:
                        w(ten, s_S, f"g1_{fprev[0]}_{fprev[1]}_{NH - 2 + n}")
                    for kd in range(KD):
                        mmk = ten.matmul(pf1[:, n % 2, :],
                                         xqTb[:, fb, kd * 128:(kd + 1) * 128],
                                         wq1v[:, kd * H + n * 512: kd * H + (n + 1) * 512],
                                         start=(kd == 0), stop=(kd == KD - 1))
                    inc(mmk, s_T)
                    mark(f"f1_{s}_{m}_{n}", s_T)
                    w(sca, s_T, f"f1_{s}_{m}_{n}")
                    if n == 0 and fprev2 is not None:
                        w(sca, s_V, f"hd_{fprev2[0]}_{fprev2[1]}")
                        w(sca, s_G, f"hg_{fprev2[0]}_{fprev2[1]}")
                    inc(sca.activation(out=hb[:, fb, n * 512:(n + 1) * 512], in_=pf1[:, n % 2, :],
                                       func=AF.Gelu, bias=ccst[:, 0:1],
                                       scale=acolT[:, seq:seq + 1]), s_S)
                    mark(f"g1_{s}_{m}_{n}", s_S)
                ft = state.get("f_tail")
                if ft is not None:
                    tail_cd(*ft)
                w(vec, s_S, f"g1_{s}_{m}_{NH - 1}")
                if fprev2 is not None:
                    w(vec, s_S, f"vp_{fprev2[0]}_{fprev2[1]}")
                c_mx, c_inv, c_su = 38 + 3 * fb, 39 + 3 * fb, 40 + 3 * fb
                vec.reduce_max(col(c_mx), hb[:, fb, :], axis=AX.X, apply_absolute_value=True)
                vec.drain()
                vec.tensor_scalar(out=col(c_mx), in0=col(c_mx), scalar1=EPS_Q, scalar2=None, op0=ALU.max)
                vec.drain()
                vec.reciprocal(col(c_inv), col(c_mx))
                vec.drain()
                vec.tensor_scalar(out=col(c_su), in0=col(c_inv), scalar1=-127.0, scalar2=None, op0=ALU.mult)
                vec.tensor_scalar(out=bcolT[:, seq:seq + 1], in0=col(c_mx),
                                  scalar1=col(C_S2), scalar2=-1.0 / 254.0, op0=ALU.mult, op1=ALU.mult)
                inc(vec.drain(), s_V)
                mark(f"hcols_{s}_{m}", s_V)
                if ft is not None:
                    tail_V2(*ft)
                state["f_tail"] = (s, m, fb)

            def emit_w1p1(c):
                b = c % 2
                if c >= 2:
                    w(sync, s_V, f"wred_0_{c - 2}")
                inc(sync.dma_start(out=wb[:, b, :], in_=w1_d[c * 128:(c + 1) * 128, :]),
                    s_w[b], 16)
                mark(f"wld_0_{c}", s_w[b])
                w(vec, s_w[b], f"wld_0_{c}")
                inc(vec.reduce_sum(wcols[:, c:c + 1], wb[:, b, :], axis=AX.X,
                                   apply_absolute_value=True), s_V)
                mark(f"wred_0_{c}", s_V)

            def emit_w2p1(c):
                if c >= 1:
                    w(sync, s_V, f"wred_1_{c - 1}")
                else:
                    w(sync, s_V, "qk_bf")
                inc(sync.dma_start(out=wb2, in_=w2_d[c * 128:(c + 1) * 128, :]), s_wb2a, 16)
                mark(f"wld_1_{c}", s_wb2a)
                w(vec, s_wb2a, f"wld_1_{c}")
                inc(vec.reduce_sum(wcols[:, 32 + c:33 + c], wb2, axis=AX.X,
                                   apply_absolute_value=True), s_V)
                mark(f"wred_1_{c}", s_V)

            def emit_scalc():
                w(vec, s_V, f"wred_0_{W1C - 1}")
                w(vec, s_V, f"wred_1_{W2C - 1}")
                vec.reduce_sum(col(C_S1), wcols[:, 0:W1C], axis=AX.X)
                vec.reduce_sum(col(C_S2), wcols[:, 32:32 + W2C], axis=AX.X)
                inc(vec.drain(), s_V)
                mark("wcolsum", s_V)
                w(ten, s_V, "wcolsum")
                w(ten, s_V, "kvev_0")
                ten.matmul(pkv[0:1, 1, 0:1], onescol[:], col(C_S1), start=True, stop=True)
                inc(ten.matmul(pkv[0:1, 1, 1:2], onescol[:], col(C_S2), start=True, stop=True), s_T)
                mark("wsred", s_T)
                w(vec, s_T, "wsred")
                vec.tensor_copy(sval[:, 0:2], pkv[0:1, 1, 0:2])
                vec.drain()
                vec.tensor_scalar(out=sval[:, 2:3], in0=sval[:, 0:1], scalar1=0.5 / (H * D),
                                  scalar2=None, op0=ALU.mult)
                vec.tensor_scalar(out=sval[:, 3:4], in0=sval[:, 1:2], scalar1=0.5 / (D * H),
                                  scalar2=None, op0=ALU.mult)
                inc(vec.drain(), s_V)
                mark("wsval", s_V)
                w(ten, s_V, "wsval")
                inc(ten.matmul(pkv[:, 1, 4:6], onesrow[:], sval[:, 2:4], start=True, stop=True), s_T)
                mark("wsbcast", s_T)
                w(vec, s_T, "wsbcast")
                vec.tensor_copy(cols[:, C_TH1P:C_TH1P + 2], pkv[:, 1, 4:6])
                vec.drain()
                vec.tensor_scalar(out=col(C_S1), in0=col(C_TH1P), scalar1=2.0, scalar2=None, op0=ALU.mult)
                vec.tensor_scalar(out=col(C_S2), in0=col(C_TH2P), scalar1=2.0, scalar2=None, op0=ALU.mult)
                vec.tensor_scalar(out=col(C_TH1N), in0=col(C_TH1P), scalar1=-1.0, scalar2=None, op0=ALU.mult)
                vec.tensor_scalar(out=col(C_TH2N), in0=col(C_TH2P), scalar1=-1.0, scalar2=None, op0=ALU.mult)
                vec.drain()
                vec.tensor_scalar(out=acolT[:, 0:TC], in0=statc[:, 0, 1, :],
                                  scalar1=col(C_S1), scalar2=-1.0 / 254.0, op0=ALU.mult, op1=ALU.mult)
                if n_seq > 1:
                    vec.tensor_scalar(out=acolT[:, TC:2 * TC], in0=statc[:, 1, 1, :],
                                      scalar1=col(C_S1), scalar2=-1.0 / 254.0, op0=ALU.mult, op1=ALU.mult)
                inc(vec.drain(), s_V)
                mark("wscols", s_V)

            def emit_wp2(wi, c):
                b = c % 2
                wd = D if wi == 0 else H
                thp = col(C_TH1P) if wi == 0 else col(C_TH2P)
                thn = col(C_TH1N) if wi == 0 else col(C_TH2N)
                s_q = s_w1q if wi == 0 else s_w2q
                scr = wq1_scr if wi == 0 else wq2_scr
                if wi == 1:
                    if c >= 1:
                        if c % 2 == 0:
                            w(sync, s_V, f"wsgV_1_{c - 1}")
                        else:
                            w(sync, s_S, f"wsgn_1_{c - 1}")
                    else:
                        w(sync, s_V, f"wred_1_{W2C - 1}")
                    inc(sync.dma_start(out=wb2, in_=w2_d[c * 128:(c + 1) * 128, :]), s_wb2b, 16)
                else:
                    w(sync, s_S, f"wsgn_0_{c - 2}" if c >= 2 else f"wsgn_1_{W2C - 1}")
                    inc(sync.dma_start(out=wb[:, b, :], in_=w1_d[c * 128:(c + 1) * 128, :]),
                        s_wp2[b], 16)
                mark(f"wld2_{wi}_{c}", s_wb2b if wi == 1 else s_wp2[b])
                vroute = (wi == 1 and c % 2 == 1)
                if not vroute:
                    w(sca, s_wb2b if wi == 1 else s_wp2[b], f"wld2_{wi}_{c}")
                    if c == 0 and wi == 1:
                        w(sca, s_V, "wscols")
                    if c >= 1:
                        w(sca, s_V, f"wtrn_{wi}_{c - 1}")
                    elif wi == 0:
                        w(sca, s_V, f"wtrn_1_{W2C - 1}")
                    src = wb[:, b, :] if wi == 0 else wb2
                    sca.activation(out=wsg[:, 0, 0:wd], in_=src, func=AF.Sign, bias=thn, scale=1.0)
                    sca.activation(out=wsg[:, 1, 0:wd], in_=src, func=AF.Sign, bias=thp, scale=1.0)
                    inc(sca.drain(), s_S)
                    mark(f"wsgn_{wi}_{c}", s_S)
                    w(vec, s_S, f"wsgn_{wi}_{c}")
                else:
                    w(vec, s_wb2b, f"wld2_{wi}_{c}")
                    vec.drain()
                    vec.tensor_scalar(out=wsg[:, 0, 0:wd], in0=wb2, scalar1=thp,
                                      scalar2=2.0, op0=ALU.is_ge, op1=ALU.mult)
                    vec.tensor_scalar(out=wsg[:, 1, 0:wd], in0=wb2, scalar1=thn,
                                      scalar2=-2.0, op0=ALU.is_lt, op1=ALU.mult)
                    inc(vec.drain(), s_V)
                    mark(f"wsgV_{wi}_{c}", s_V)
                    mark(f"wsgn_{wi}_{c}", s_S)
                if c >= 2:
                    w(vec, s_q[b], f"wqw_{wi}_{c - 2}")
                elif wi == 0:
                    w(vec, s_w2q[(W2C - 2 + c) % 2], f"wqw_1_{W2C - 2 + c}")
                inc(vec.tensor_tensor(out=wqo[:, b, 0:wd], in0=wsg[:, 0, 0:wd],
                                      in1=wsg[:, 1, 0:wd], op=ALU.add), s_V)
                mark(f"wtrn_{wi}_{c}", s_V)
                w(gp, s_V, f"wtrn_{wi}_{c}")
                inc(gp.dma_start(out=scr[c * 128:(c + 1) * 128, 0:wd], in_=wqo[:, b, 0:wd]),
                    s_q[b], 16)
                mark(f"wqw_{wi}_{c}", s_q[b])

            def emit_wq1T():
                w(sync, s_w1q[0], f"wqw_0_{W1C - 2}")
                w(sync, s_w1q[1], f"wqw_0_{W1C - 1}")
                w(sync, s_w2q[0], f"wqw_1_{W2C - 2}")
                w(sync, s_w2q[1], f"wqw_1_{W2C - 1}")
                wq1v = wqT.rearrange("p k h -> p (k h)")
                for kd in range(KD):
                    inc(sync.dma_start_transpose(out=wq1v[:, kd * H:(kd + 1) * H],
                                                 in_=wq1_scr[:, kd * 128:(kd + 1) * 128]),
                        s_wT, 16)
                mark("wq1T", s_wT)

            def emit_P2(s, m):
                seq = s * TC + m
                b = m % 2
                prev = divmod(seq - 1, TC) if seq >= 1 else None
                prev2 = divmod(seq - 2, TC) if seq >= 2 else None
                if prev2 is not None:
                    w(sync, s_T, f"f2_{prev2[0]}_{prev2[1]}")
                    w(sync, s_V, f"ep2_{prev2[0]}_{prev2[1]}")
                if seq == 0:
                    w(sync, s_V, f"hq_{n_seq - 1}_{TC - 1}")
                    w(sync, s_G, f"hg_{n_seq - 1}_{TC - 1}")
                    w(sync, s_S, f"vp_{n_seq - 1}_{TC - 1}")
                w(sync, s_hqw[_fpar(s, m)], f"hqw_{s}_{m}")
                inc(sync.dma_start_transpose(out=hqTb[:, b, :],
                                             in_=hq_scr[s, m].rearrange("k t c -> (k t) c")),
                    s_hqT[b], 16)
                mark(f"hqT_{s}_{m}", s_hqT[b])
                w(sync, s_qw[b], f"qw_{s}_{m}")
                inc(sync.dma_start_transpose(out=qTb[:, b, :],
                                             in_=q_scr[s, m].rearrange("k t c -> (k t) c")),
                    s_qT[b], 16)
                mark(f"qT_{s}_{m}", s_qT[b])
                xb_load(s_x2, "P2", s, m, x_d[s, m * 128:(m + 1) * 128, :])
                mark(f"x2_{s}_{m}", s_x2[b])
                pb = seq % 2
                pnum0 = ppq[:] if pb == 0 else pf1[:, 0, :]
                pnum1 = ppk[:] if pb == 0 else pf1[:, 1, :]
                pz = pkv[:, 1, 8:9] if pb == 0 else pkv[:, 1, 520:521]
                w(ten, s_qT[b], f"qT_{s}_{m}")
                if seq == 0:
                    w(ten, s_V, f"kvev_{n_seq - 1}")
                if prev2 is not None:
                    w(ten, s_V, f"ep1_{prev2[0]}_{prev2[1]}")
                if seq == 1:
                    sl, sm = state["f_prev"][-1]
                    w(ten, s_S, f"g1_{sl}_{sm}_{NH - 1}")
                if pb == 1 and prev is not None:
                    w(ten, s_V, f"ep2_{prev[0]}_{prev[1]}")
                for ec in range(EC):
                    qt = qTb[:, b, ec * 128:(ec + 1) * 128]
                    ten.matmul(pnum0, qt, kvb[:, s, ec, 0:512],
                               start=(ec == 0), stop=(ec == EC - 1))
                    ten.matmul(pnum1, qt, kvb[:, s, ec, 512:1024],
                               start=(ec == 0), stop=(ec == EC - 1))
                    mmk = ten.matmul(pz, qt, kvb[:, s, ec, 1024:1025],
                                     start=(ec == 0), stop=(ec == EC - 1))
                inc(mmk, s_T)
                mark(f"num_{s}_{m}", s_T)
                w(ten, s_hqT[b], f"hqT_{s}_{m}")
                if seq == 0:
                    sl, sm = state["f_prev"][-1]
                    w(ten, s_S, f"g1_{sl}_{sm}_{NH - 1}")
                if prev2 is not None:
                    w(ten, s_V, f"ep2_{prev2[0]}_{prev2[1]}")
                if pb == 0 and prev is not None:
                    w(ten, s_V, f"ep1_{prev[0]}_{prev[1]}")
                pf2 = [pf1[:, 0, :], pf1[:, 1, :]] if (pb == 0) else \
                      [pkv[:, 0, 0:512], pkv[:, 0, 512:1024]]
                for kh in range(KH):
                    if seq == 0 and kh % (KH // 4) == 0:
                        g2 = kh // (KH // 4)
                        w(ten, s_w2T[g2], f"wq2Tg_{g2}")
                    ht = hqTb[:, b, kh * 128:(kh + 1) * 128]
                    for half in range(2):
                        mmk = ten.matmul(pf2[half], ht, wqT[:, kh, half * 512:(half + 1) * 512],
                                         start=(kh == 0), stop=(kh == KH - 1))
                inc(mmk, s_T)
                mark(f"f2_{s}_{m}", s_T)
                w(vec, s_T, f"num_{s}_{m}")
                w(vec, s_x2[b], f"x2_{s}_{m}")
                if prev is not None:
                    w(vec, s_st[prev[1] % 2], f"st_{prev[0]}_{prev[1]}")
                zr = zrb[:, pb:pb + 1]
                vec.drain()
                vec.tensor_scalar(out=zr, in0=pz, scalar1=EPS_Z,
                                  scalar2=None, op0=ALU.add)
                vec.drain()
                vec.reciprocal(zr, zr)
                vec.drain()
                vec.scalar_tensor_tensor(out=ob[:, 0:512], in0=pnum0, scalar=zr,
                                         in1=xb[:, b, 0:512], op0=ALU.mult, op1=ALU.add)
                inc(vec.scalar_tensor_tensor(out=ob[:, 512:1024], in0=pnum1, scalar=zr,
                                             in1=xb[:, b, 512:1024], op0=ALU.mult, op1=ALU.add), s_V)
                mark(f"ep1_{s}_{m}", s_V)
                w(vec, s_T, f"f2_{s}_{m}")
                bc = bcolT[:, seq:seq + 1]
                vec.drain()
                vec.scalar_tensor_tensor(out=ob[:, 0:512], in0=pf2[0], scalar=bc,
                                         in1=ob[:, 0:512], op0=ALU.mult, op1=ALU.add)
                inc(vec.scalar_tensor_tensor(out=ob[:, 512:1024], in0=pf2[1], scalar=bc,
                                             in1=ob[:, 512:1024], op0=ALU.mult, op1=ALU.add), s_V)
                mark(f"ep2_{s}_{m}", s_V)
                w(gp, s_V, f"ep2_{s}_{m}")
                inc(gp.dma_start(out=out_d[s, m * 128:(m + 1) * 128, :], in_=ob[:]),
                    s_st[b], 16)
                mark(f"st_{s}_{m}", s_st[b])

            vec.memset(ccst[:, 0:1], 0.0)
            vec.memset(ccst[:, 1:2], 0.5)
            vec.memset(ccst[:, 2:3], 1.0)
            vec.memset(ccst[:, 3:4], -0.5)
            vec.memset(onescol[:], 1.0)
            vec.memset(onesrow[:], 1.0)
            inc(vec.drain(), s_V)
            mark("consts", s_V)
            w(sca, s_V, "consts")
            w(ten, s_V, "consts")
            w(gp, s_V, "consts")
            inc(sync.dma_start(out=wb2.rearrange("t (k e) -> t k e", k=KD)[:, :, 0:E],
                               in_=qp_d.rearrange("(k t) e -> t k e", k=KD)), s_qk, 16)
            inc(sync.dma_start(out=wb2.rearrange("t (k e) -> t k e", k=KD)[:, :, E:2 * E],
                               in_=kp_d.rearrange("(k t) e -> t k e", k=KD)), s_qk, 16)
            mark("qk_ld", s_qk)
            w(vec, s_qk, "qk_ld")
            vec.tensor_copy(qkb[:, 0, :, :], wb2.rearrange("t (k e) -> t k e", k=KD)[:, :, 0:E])
            vec.tensor_copy(qkb[:, 1, :, :], wb2.rearrange("t (k e) -> t k e", k=KD)[:, :, E:2 * E])
            inc(vec.drain(), s_V)
            mark("qk_bf", s_V)

            def runjob(j):
                kind = j[0]
                if kind == "w1p1":
                    emit_w1p1(j[1])
                elif kind == "w2p1":
                    emit_w2p1(j[1])
                elif kind == "wp2":
                    emit_wp2(j[1], j[2])
                elif kind == "A0":
                    emit_A0(j[1], j[2])
                elif kind == "A1a":
                    emit_A1a(j[1], j[2])
                elif kind == "A1c":
                    emit_A1c(j[1], j[2])
                elif kind == "A1b":
                    emit_A1bS(j[1], j[2])
                    emit_A1bT(j[1], j[2])
                elif kind == "batch":
                    emit_batch(j[1])

            wjobs = [("w1p1", c) for c in range(W1C)] + [("w2p1", c) for c in range(W2C)]
            ajobs = [("A0", 0, m) for m in range(TC)] + [("batch", 0)]
            for m in range(TC):
                if m >= 2:
                    ajobs.append(("A1b", 0, m - 2))
                ajobs.append(("A1a", 0, m))
                if m >= 1:
                    ajobs.append(("A1c", 0, m - 1))
                ajobs.append(("A0", 1, m))
            ajobs += [("A1b", 0, TC - 2), ("A1c", 0, TC - 1),
                      ("A1b", 0, TC - 1), ("batch", 1)]
            wi_ = ai_ = 0
            while wi_ < len(wjobs) or ai_ < len(ajobs):
                if ai_ < len(ajobs):
                    runjob(ajobs[ai_]); ai_ += 1
                if wi_ < len(wjobs):
                    runjob(wjobs[wi_]); wi_ += 1
            emit_kvev(0)
            emit_scalc()
            wjobs = [("wp2", 1, c) for c in range(W2C)] + [("wp2", 0, c) for c in range(W1C)]
            ajobs = []
            for m in range(TC):
                if m >= 2:
                    ajobs.append(("A1b", 1, m - 2))
                ajobs.append(("A1a", 1, m))
                if m >= 1:
                    ajobs.append(("A1c", 1, m - 1))
                ajobs += [("A0", 2, m), ("A0", 3, m)]
            ajobs += [("A1b", 1, TC - 2), ("A1c", 1, TC - 1),
                      ("A1b", 1, TC - 1), ("batch", 2), ("batch", 3)]
            wi_ = ai_ = 0
            while wi_ < len(wjobs) or ai_ < len(ajobs):
                if ai_ < len(ajobs):
                    runjob(ajobs[ai_]); ai_ += 1
                if wi_ < len(wjobs):
                    runjob(wjobs[wi_]); wi_ += 1
            emit_kvev(1)
            emit_wq1T()
            for m in range(TC):
                emit_FFN1(0, m)
                emit_FFN1(1, m)
            NK = 2 * TC
            a1map = lambda k: (2 + k // TC, k % TC)
            emit_A1a(*a1map(0))
            if NK > 1:
                emit_A1a(*a1map(1))
            emit_A1c(*a1map(0))
            emit_A1bS(*a1map(0))
            emit_A1bT(*a1map(0))
            if NK > 2:
                emit_A1a(*a1map(2))
            if NK > 1:
                emit_A1c(*a1map(1))
            for k in range(NK):
                if k + 1 < NK:
                    emit_A1bS(*a1map(k + 1))
                emit_FFN1(*a1map(k))
                if k + 1 < NK:
                    emit_A1bT(*a1map(k + 1))
                    if (k + 1) % TC == TC - 1:
                        emit_kvev(2 + (k + 1) // TC)
                if k + 2 < NK:
                    emit_A1c(*a1map(k + 2))
                if k + 3 < NK:
                    emit_A1a(*a1map(k + 3))
            flush_tail()
            w(sync, s_T, f"f1_3_{TC - 1}_{NH - 1}")
            for kh in range(KH):
                g2 = kh // (KH // 4)
                inc(sync.dma_start_transpose(out=wqT[:, kh, :],
                                             in_=wq2_scr[:, kh * 128:(kh + 1) * 128]),
                    s_w2T[g2], 16)
                if kh % (KH // 4) == KH // 4 - 1:
                    mark(f"wq2Tg_{g2}", s_w2T[g2])
            for s in range(n_seq):
                for m in range(TC):
                    emit_P2(s, m)
            w(sync, s_st[0], f"st_{n_seq - 1}_{TC - 2}")
            w(sync, s_st[1], f"st_{n_seq - 1}_{TC - 1}")

        fake = _Proxy(None)

        @blk.sync
        def _(sync):
            run_schedule(_Proxy(sync), fake, fake, fake, fake)

        @blk.vector
        def _(vector):
            run_schedule(fake, _Proxy(vector), fake, fake, fake)

        @blk.scalar
        def _(scalar):
            run_schedule(fake, fake, _Proxy(scalar), fake, fake)

        @blk.tensor
        def _(tensor):
            run_schedule(fake, fake, fake, _Proxy(tensor), fake)

        @blk.gpsimd
        def _(gpsimd):
            run_schedule(fake, fake, fake, fake, _Proxy(gpsimd))

    return nc


_NC_CACHE = {}


def _get_nc(n_seq, T):
    key = (n_seq, T)
    if key not in _NC_CACHE:
        _NC_CACHE[key] = build_nc(n_seq, T)
    return _NC_CACHE[key]


def kernel(L0_attn_x, L0_attn_q_proj, L0_attn_k_proj, L0_ffn_w1, L0_ffn_w2):
    x = np.ascontiguousarray(np.asarray(L0_attn_x, dtype=np.float32))
    qp = np.ascontiguousarray(np.asarray(L0_attn_q_proj, dtype=np.float32))
    kp = np.ascontiguousarray(np.asarray(L0_attn_k_proj, dtype=np.float32))
    w1 = np.ascontiguousarray(np.asarray(L0_ffn_w1, dtype=np.float32))
    w2 = np.ascontiguousarray(np.asarray(L0_ffn_w2, dtype=np.float32))
    B, T, d = x.shape
    n_seq = B // N_CORES
    nc = _get_nc(n_seq, T)
    in_maps = []
    for c in range(N_CORES):
        in_maps.append({
            "x": x[c * n_seq:(c + 1) * n_seq],
            "q_proj": qp, "k_proj": kp, "w1": w1, "w2": w2,
        })
    res = run_bass_kernel_spmd(nc, in_maps, core_ids=list(range(N_CORES)))
    return np.concatenate([np.asarray(r["out"]) for r in res.results], axis=0)
